# version 01
# baseline (speedup 1.0000x reference)
import numpy as np
from ml_dtypes import bfloat16
import concourse.bass as bass
import concourse.mybir as mybir
from contextlib import ExitStack

F32 = mybir.dt.float32
BF16 = mybir.dt.bfloat16
FP16 = mybir.dt.float16
U32 = mybir.dt.uint32
AF = mybir.ActivationFunctionType
ALU = mybir.AluOpType
AX = mybir.AxisListType

S_FULL, S_OWN, D, H, HD, NB, CB = 4096, 2048, 1024, 16, 64, 256, 16
SCALE = float(1.0 / np.sqrt(HD))
C_SHIFT = 10.0
HIDDEN = 4096
DELTA = 1e-9


class Sched:
    ENGS = ("sync", "gpsimd", "tensor", "vector", "scalar")

    def __init__(self):
        self.q = {e: [] for e in self.ENGS}

    def on(self, eng):
        def deco(fn):
            self.q[eng].append(fn)
        return deco

    def emit(self, blk):
        for ename in self.ENGS:
            fns = self.q[ename]
            dec = getattr(blk, ename)
            @dec
            def _(engine, fns=fns):
                for fn in fns:
                    fn(engine)


class DmaQ:

    def __init__(self, sems):
        self.sems = sems
        self.counts = [0] * len(sems)
        self.ep = -1
        self.first = False

    def epoch(self):
        self.ep += 1
        self.first = True

    def _pre(self):
        s_i = self.ep % len(self.sems)
        sem = self.sems[s_i]
        pre = (sem, self.counts[s_i] * 16) if (self.first and self.counts[s_i]) else None
        self.first = False
        self.counts[s_i] += 1
        return sem, pre

    def plan(self, out_ap, in_ap, transpose=False):
        sem, pre = self._pre()

        def emit(e, out_ap=out_ap, in_ap=in_ap, pre=pre, sem=sem, tr=transpose):
            if pre is not None:
                e.wait_ge(pre[0], pre[1])
            if tr:
                e.dma_start_transpose(out_ap, in_ap).then_inc(sem, 16)
            else:
                e.dma_start(out=out_ap, in_=in_ap).then_inc(sem, 16)
        return emit

    def plan_indirect(self, out_ap, in_ap, idx_ap):
        sem, pre = self._pre()

        def emit(e, out_ap=out_ap, in_ap=in_ap, idx_ap=idx_ap, pre=pre, sem=sem):
            if pre is not None:
                e.wait_ge(pre[0], pre[1])
            e.indirect_dma_start(
                out=out_ap, out_offset=None, in_=in_ap,
                in_offset=bass.IndirectOffsetOnAxis(ap=idx_ap, axis=0),
            ).then_inc(sem, 16)
        return emit

    def close(self):
        s_i = self.ep % len(self.sems)
        return (self.sems[s_i], self.counts[s_i] * 16)

    @staticmethod
    def wait(e, tok):
        e.wait_ge(tok[0], tok[1])


class Arena:
    def __init__(self, ap, n_cols, name=""):
        self.base_ap = ap
        self.n = n_cols
        self.top = 0
        self.marks = {}
        self.name = name
        self.hi = 0

    ALIGN = 32

    def alloc(self, cols, tag="", rows=128):
        al = self.ALIGN if cols >= 32 else 4
        a = (self.top + al - 1) // al * al
        assert a + cols <= self.n, f"arena {self.name} overflow: {tag} +{cols} at {a}/{self.n}"
        self.top = a + cols
        self.hi = max(self.hi, self.top)
        return self.base_ap[0:rows, a:a + cols]

    def mark(self, key):
        self.marks[key] = self.top

    def reset(self, key):
        self.top = self.marks[key]



def host_consts_l0(hf):
    f32 = np.float32
    if hf == 0:
        abspos = np.concatenate([np.arange(2048, 4096), np.arange(0, 2048)]).astype(f32)
    else:
        abspos = np.arange(4096).astype(f32)
    half = HD // 2
    inv = 1.0 / 10000.0 ** (np.arange(half, dtype=f32) * f32(2.0 / HD))
    freq_of_p = inv[(np.arange(128) % 64) // 2]
    ang = abspos[None, :] * freq_of_p[:, None]
    cs1 = np.cos(ang).astype(np.float16)
    sgn = np.where(np.arange(128) % 2 == 0, -1.0, 1.0).astype(f32)
    cs2 = (np.sin(ang) * sgn[:, None]).astype(np.float16)
    qpos = abspos[2048:]
    maskwin = np.zeros((4, 6, 128, 512), np.float32)
    for qb in range(4):
        kt0 = 2048 + 512 * qb - 256
        for c in range(6):
            base = kt0 + 128 * c
            ktpos = abspos[base:base + 128]
            qq = qpos[qb * 512:(qb + 1) * 512]
            maskwin[qb, c] = (ktpos[:, None] <= qq[None, :]) & (ktpos[:, None] > qq[None, :] - 256)
    blk_abs = abspos[np.arange(NB) * CB]
    maskcomp = np.zeros((4, 2, 128, 512), np.float32)
    for qb in range(4):
        qq = qpos[qb * 512:(qb + 1) * 512]
        valid = (blk_abs[:, None] + CB) <= qq[None, :]
        maskcomp[qb, 0] = valid[:128]
        maskcomp[qb, 1] = valid[128:]
    return {
        "cs1": cs1, "cs2": cs2,
        "maskwin": np.ascontiguousarray(maskwin.astype(bfloat16)),
        "maskcomp": np.ascontiguousarray(maskcomp.astype(bfloat16)),
        "qpos16": np.ascontiguousarray(np.broadcast_to(qpos[None, :], (16, 2048)).astype(f32)),
        "blkabs_row": blk_abs[None, :].astype(f32).copy(),
        "iota256": np.arange(256, dtype=f32)[None, :].copy(),
        "selpat0": np.concatenate([np.ones(8), np.zeros(8)]).astype(f32)[None, :].copy(),
        "selpat1": np.concatenate([np.zeros(8), np.ones(8)]).astype(f32)[None, :].copy(),
        "iota8x2": np.concatenate([np.arange(8), np.arange(8)]).astype(f32)[:, None].copy(),
        "delta8": np.ascontiguousarray(
            np.broadcast_to((np.arange(8) * DELTA).astype(f32)[None, :], (128, 8))),
        "ident": np.eye(128, dtype=f32),
        "pswap": np.ascontiguousarray(np.eye(128, dtype=f32)[:, np.arange(128) ^ 1]),
    }


def host_weights_l0(ins):
    f32 = np.float32
    wk = ins["L0_attn_wk_comp"].astype(f32)
    wv = ins["L0_attn_wv_comp"].astype(f32)
    wck = np.zeros((128, 8), f32)
    wcv = np.zeros((128, 8), f32)
    for p in range(128):
        wck[p, p // 16] = wk[p % 16]
        wcv[p, p // 16] = wv[p % 16]
    b1 = ins["L0_ffn_b1"].astype(f32)
    b1ct = np.ascontiguousarray(b1.reshape(8, 32, 128).transpose(2, 0, 1).reshape(128, 256))
    bo_ct = np.ascontiguousarray(ins["L0_attn_bo"].astype(f32).reshape(8, 128).T)
    return {
        "wq": ins["L0_attn_Wq"].astype(f32), "wkm": ins["L0_attn_Wk"].astype(f32),
        "wvm": ins["L0_attn_Wv"].astype(f32), "wo": ins["L0_attn_Wo"].astype(f32),
        "bo_ct": bo_ct, "wcomp_k": wck, "wcomp_v": wcv,
        "pe16": ins["L0_attn_pe"].astype(f32),
        "wk_col": wk[:, None].copy(), "wv_col": wv[:, None].copy(),
        "wg": ins["L0_attn_Wg"].astype(f32),
        "bg3": ins["L0_attn_bg"].astype(f32)[:, None].copy(),
        "wr": ins["L0_ffn_Wr"].astype(f32),
        "br8": ins["L0_ffn_br"].astype(f32)[:, None].copy(),
        "w1": ins["L0_ffn_W1"].astype(f32), "b1ct": b1ct,
        "w2": ins["L0_ffn_W2"].astype(f32), "b2": ins["L0_ffn_b2"].astype(f32),
    }


def perm_x(x_b, hf):
    if hf == 1:
        return np.ascontiguousarray(x_b)
    return np.ascontiguousarray(np.concatenate([x_b[2048:], x_b[:2048]], axis=0))


def l0_in_map(ins, be, hf):
    m = {"x_perm": perm_x(ins["L0_attn_x"][be].astype(np.float32), hf)}
    m.update(host_weights_l0(ins))
    m.update(host_consts_l0(hf))
    return m



SEM_NAMES = ("dA", "dB", "dC", "eA", "eB", "eC",
             "m_pj", "m_pm", "m_rg", "m_tr", "m_cc", "m_im", "m_sc", "m_av",
             "m_bc", "m_wo", "m_f1", "m_f2", "m_ps",
             "c_ln", "c_pj", "c_pm", "c_rg", "c_cc", "c_ex", "c_av", "c_h", "c_ms",
             "c_sl", "v_ln", "v_rp", "v_ms", "v_tm", "v_ep", "v_ff", "v_cc", "v_fin")


def build_l0(stage="F"):
    nc = bass.Bass("TRN2", target_bir_lowering=False)
    dram = lambda name, shape, dt=F32, kind="ExternalInput": \
        nc.dram_tensor(name, shape, dt, kind=kind).ap()

    x_in = dram("x_perm", [S_FULL, D])
    wq_d = dram("wq", [D, D]); wk_d = dram("wkm", [D, D]); wv_d = dram("wvm", [D, D])
    wo_d = dram("wo", [D, D]); bo_d = dram("bo_ct", [128, 8])
    wck_d = dram("wcomp_k", [128, 8]); wcv_d = dram("wcomp_v", [128, 8])
    wkc_d = dram("wk_col", [16, 1]); wvc_d = dram("wv_col", [16, 1])
    pe_d = dram("pe16", [16, D])
    wg_d = dram("wg", [D, 3]); bg_d = dram("bg3", [3, 1])
    wr_d = dram("wr", [D, 8]); br_d = dram("br8", [8, 1])
    w1_d = dram("w1", [8, D, HIDDEN]); b1_d = dram("b1ct", [128, 256])
    w2_d = dram("w2", [8, HIDDEN, D]); b2_d = dram("b2", [8, D])
    cs1_d = dram("cs1", [128, S_FULL], FP16); cs2_d = dram("cs2", [128, S_FULL], FP16)
    mw_d = dram("maskwin", [4, 6, 128, 512], BF16)
    mc_d = dram("maskcomp", [4, 2, 128, 512], BF16)
    qp_d = dram("qpos16", [16, S_OWN])
    blkabs_d = dram("blkabs_row", [1, NB])
    iota256_d = dram("iota256", [1, NB])
    selpat0_d = dram("selpat0", [1, 16])
    selpat1_d = dram("selpat1", [1, 16])
    iota8_d = dram("iota8x2", [16, 1])
    delta8_d = dram("delta8", [128, 8])
    ident_d = dram("ident", [128, 128])
    pswap_d = dram("pswap", [128, 128])

    x1_out = dram("x1", [S_OWN, D], F32, kind="ExternalOutput")

    scrkind = "ExternalOutput" if stage == "A" else "Internal"
    xn_dram = dram("xn_scr", [S_FULL, D], BF16, kind=scrkind)
    kfm_dram = dram("kfm_scr", [D, S_FULL], BF16, kind=scrkind)
    vfm_dram = dram("vfm_scr", [D, S_FULL], BF16, kind=scrkind)
    ktm_dram = dram("ktm_scr", [S_FULL, D], BF16, kind="Internal")
    vtm_dram = dram("vtm_scr", [S_FULL, 1040], BF16, kind=scrkind)
    ck_dram = dram("ck_scr", [NB, D], BF16, kind=scrkind)
    cv_dram = dram("cv_scr", [NB, 1040], BF16, kind="Internal")
    sfm_dram = dram("sfm_scr", [D, S_OWN], BF16,
                    kind="ExternalOutput" if stage == "C" else "Internal")
    gffn_dram = dram("gffn_scr", [S_OWN, D], BF16, kind="Internal")
    dbg = {}
    if stage == "A":
        dbg["imp"] = dram("d_imp", [1, NB], F32, kind="ExternalOutput")
        dbg["pos"] = dram("d_pos", [2, 16], F32, kind="ExternalOutput")
        dbg["qbar"] = dram("d_qbar", [128, 8], F32, kind="ExternalOutput")
        dbg["rw"] = dram("d_rw", [128, 128], F32, kind="ExternalOutput")
        dbg["g3"] = dram("d_g3", [128, 48], F32, kind="ExternalOutput")
        dbg["selk"] = dram("d_selk", [16, D], F32, kind="ExternalOutput")
        dbg["wvb"] = dram("d_wvb", [128, 1024], F32, kind="ExternalOutput")
        dbg["xnf"] = dram("d_xnf", [128, 512], F32, kind="ExternalOutput")

    with ExitStack() as stack:
        ec = stack.enter_context
        A16 = Arena(ec(nc.sbuf_tensor("arena16", [128, 76416], BF16)).ap(), 76416, "bf")
        A32 = Arena(ec(nc.sbuf_tensor("arena32", [128, 8400], F32)).ap(), 8400, "f32")
        AH = Arena(ec(nc.sbuf_tensor("arenah", [128, 8192], FP16)).ap(), 8192, "fp16")

        idbf = A16.alloc(128, "ident")
        pswap = A16.alloc(128, "pswap")
        wck_b = A16.alloc(8, "wck"); wcv_b = A16.alloc(8, "wcv")
        wkcol_b = A16.alloc(1, "wkcol", rows=16); wvcol_b = A16.alloc(1, "wvcol", rows=16)
        wg_b = A16.alloc(24, "wg"); wr_b = A16.alloc(64, "wr")
        onesrow = A16.alloc(64, "ones1x64", rows=1)
        rwbf = A16.alloc(128, "rwbf")
        r_bf = A16.alloc(512, "r_bf", rows=1)
        gr_bf = A16.alloc(512, "gr", rows=1)
        qbar_bf = A16.alloc(8, "qbar_bf")
        A16.mark("persist")
        cs1_sb = AH.alloc(S_FULL, "cs1")
        cs2_sb = AH.alloc(S_FULL, "cs2")

        id32 = A32.alloc(128, "id32")
        bo_sb = A32.alloc(8, "bo"); b1_sb = A32.alloc(256, "b1")
        bg_sb = A32.alloc(1, "bg", rows=3); br_sb = A32.alloc(1, "br", rows=8)
        qpos_sb = A32.alloc(S_OWN, "qpos16", rows=16)
        blkabs_sb = A32.alloc(NB, "blkabs", rows=1)
        iota256_sb = A32.alloc(NB, "iota256", rows=1)
        selpat0_sb = A32.alloc(16, "selpat0", rows=1)
        selpat1_sb = A32.alloc(16, "selpat1", rows=1)
        iota8_sb = A32.alloc(1, "iota8x2", rows=16)
        delta8_sb = A32.alloc(8, "delta8")
        g_tm = A32.alloc(48, "g_tm")
        rw_tm = A32.alloc(128, "rw_tm")
        sc_tm = A32.alloc(128, "sc_tm")
        gl_tm = A32.alloc(48, "gl_tm")
        qbarp = A32.alloc(64, "qbarp")
        qbar_sb = A32.alloc(8, "qbar")
        imp_sb = A32.alloc(NB, "imp", rows=1)
        scrA = A32.alloc(NB, "scrA", rows=1)
        scrB = A32.alloc(NB, "scrB", rows=1)
        s1a = A32.alloc(1, "s1a", rows=1)
        s1b = A32.alloc(1, "s1b", rows=1)
        s1c = A32.alloc(1, "s1c", rows=1)
        s1d = A32.alloc(1, "s1d", rows=1)
        scrC = A32.alloc(NB, "scrC", rows=1)
        posperm = A32.alloc(1, "posperm", rows=16)
        posabs = A32.alloc(1, "posabs", rows=16)
        den_g = A32.alloc(512, "deng", rows=1)
        mvst = [A32.alloc(12, f"lnstats{i}") for i in range(2)]
        mv_sb = [A32.alloc(2, f"mv{i}") for i in range(2)]
        rstd_sb = [A32.alloc(1, f"rstd{i}") for i in range(2)]
        eps_sb = A32.alloc(1, "eps")
        negC_sb = A32.alloc(1, "negC")
        nmx_sb = A32.alloc(1, "nmx")
        r8_sb = A32.alloc(1, "r8")
        m8 = A32.alloc(8, "m8")
        k8a = A32.alloc(8, "k8a"); k8b = A32.alloc(8, "k8b"); k8c = A32.alloc(8, "k8c")
        posu = ec(nc.sbuf_tensor("posu", [16, 1], U32)).ap()

        q_own = A16.alloc(8 * S_OWN, "q_own")
        ck_fm = A16.alloc(8 * NB, "ck_fm")
        cv_ext = A16.alloc(2 * 1040, "cv_ext")
        selk_tm = A16.alloc(1024, "selk_tm", rows=16)
        selv_ext = A16.alloc(1040, "selv_ext", rows=16)
        selk_fm = A16.alloc(8 * 16, "selk_fm")
        A16.mark("base"); A32.mark("base")

        pp = [ec(nc.psum_tensor(f"pp{i}", [128, 512], F32)) for i in range(2)]
        pq = [ec(nc.psum_tensor(f"pq{i}", [128, 512], F32)) for i in range(2)]
        pav = [ec(nc.psum_tensor(f"pav{i}", [128, 512], F32)) for i in range(4)]

        SEMS = {s_name: ec(nc.semaphore(s_name)) for s_name in SEM_NAMES}
        blk = ec(nc.Block())

        S = Sched()
        Lsw = DmaQ([SEMS["dA"], SEMS["dB"], SEMS["dC"]])
        Lhw = DmaQ([SEMS["eA"], SEMS["eB"], SEMS["eC"]])
        CN = {k: 0 for k in SEM_NAMES if k[0] in "mcv" and "_" in k}
        on = S.on

        def inc(k, d=1):
            CN[k] += d
            return CN[k]

        def hw_snap():
            return [c * 16 for c in Lhw.counts]

        def sw_snap():
            return [c * 16 for c in Lsw.counts]

        def emit_fence(e, sems_list, snap):
            for sem, val in zip(sems_list, snap):
                if val:
                    e.wait_ge(sem, val)

        Lsw.epoch()
        swc = [Lsw.plan(idbf, ident_d), Lsw.plan(pswap, pswap_d),
               Lsw.plan(wck_b, wck_d), Lsw.plan(wcv_b, wcv_d),
               Lsw.plan(wg_b.rearrange("p (c g) -> p c g", g=3),
                        wg_d.rearrange("(c p) g -> p c g", p=128)),
               Lsw.plan(wr_b.rearrange("p (c g) -> p c g", g=8),
                        wr_d.rearrange("(c p) g -> p c g", p=128)),
               Lsw.plan(wkcol_b, wkc_d), Lsw.plan(wvcol_b, wvc_d)]
        tok_swc = Lsw.close()

        @on("gpsimd")
        def _(e, plan=swc):
            for p in plan:
                p(e)

        Lhw.epoch()
        hwc = [Lhw.plan(cs1_sb, cs1_d), Lhw.plan(cs2_sb, cs2_d),
               Lhw.plan(id32, ident_d),
               Lhw.plan(bo_sb, bo_d), Lhw.plan(b1_sb, b1_d),
               Lhw.plan(bg_sb, bg_d), Lhw.plan(br_sb, br_d),
               Lhw.plan(qpos_sb, qp_d), Lhw.plan(blkabs_sb, blkabs_d),
               Lhw.plan(iota256_sb, iota256_d), Lhw.plan(selpat0_sb, selpat0_d),
               Lhw.plan(selpat1_sb, selpat1_d),
               Lhw.plan(iota8_sb, iota8_d), Lhw.plan(delta8_sb, delta8_d)]
        tok_hwc = Lhw.close()

        @on("sync")
        def _(e, plan=hwc):
            for p in plan:
                p(e)

        @on("vector")
        def _(e):
            e.memset(onesrow, 1.0)
            e.memset(eps_sb, 1e-5)
            e.memset(negC_sb, -C_SHIFT)
            e.drain()
            e.nop().then_inc(SEMS["v_ms"], 1)
        ms_ready = inc("v_ms")

        A16.mark("A"); A32.mark("A")
        wqb = A16.alloc(8 * 1024, "wq")
        wkb = A16.alloc(8 * 1024, "wk")
        wvb = A16.alloc(8 * 1024, "wv")
        xt32 = [A32.alloc(1024, f"xt32_{i}") for i in range(2)]
        xn_tmb = [A16.alloc(1024, f"xntm{i}") for i in range(2)]
        xnf = [A16.alloc(8 * 512, f"xnf{i}") for i in range(2)]
        kst8 = A16.alloc(8 * 512, "kst8")
        ksw8 = A16.alloc(8 * 512, "ksw8")
        qsw8 = ksw8
        qst8 = A16.alloc(8 * 512, "qst8")
        vst2 = [A16.alloc(512, f"vst2_{i}") for i in range(2)]
        t1_b = A16.alloc(512, "t1")
        t2_b = A16.alloc(512, "t2")
        qscr = A16.alloc(512, "qscr")
        rsc_fm = A32.alloc(512, "rsc_fm", rows=8)
        glog_fm = A32.alloc(512, "glog_fm", rows=3)

        Lsw.epoch()
        sww = []
        for kc in range(8):
            sww.append(Lsw.plan(wkb[:, kc * 1024:(kc + 1) * 1024], wk_d[kc * 128:(kc + 1) * 128, :]))
            sww.append(Lsw.plan(wvb[:, kc * 1024:(kc + 1) * 1024], wv_d[kc * 128:(kc + 1) * 128, :]))
            sww.append(Lsw.plan(wqb[:, kc * 1024:(kc + 1) * 1024], wq_d[kc * 128:(kc + 1) * 128, :]))
        tok_sww = Lsw.close()

        @on("gpsimd")
        def _(e, plan=sww):
            for p in plan:
                p(e)

        prev_tok_xs = [None, None]
        prev_tok_k = [None]
        prev_vtoks = []
        all_vtoks = []
        tile_xs_toks = []
        for u in range(32):
            T, sub = divmod(u, 4)
            Lhw.epoch()
            pxl = Lhw.plan(xt32[u % 2], x_in[u * 128:(u + 1) * 128, :])
            tok_x = Lhw.close()

            @on("sync")
            def _(e, p=pxl, need=2 * max(0, u - 1)):
                if need:
                    e.wait_ge(SEMS["v_ln"], need)
                p(e)

            @on("vector")
            def _(e, u=u, tok=tok_x, buf=xt32[u % 2], ms=ms_ready, cl=CN["c_ln"]):
                Lhw.wait(e, tok)
                e.wait_ge(SEMS["v_ms"], ms)
                if cl:
                    e.wait_ge(SEMS["c_ln"], cl - 1)
                e.bn_stats(mvst[u % 2][:, 0:6], buf[:, 0:512])
                e.bn_stats(mvst[u % 2][:, 6:12], buf[:, 512:1024])
                e.drain()
                e.bn_aggr(mv_sb[u % 2], mvst[u % 2])
                e.drain()
                e.nop().then_inc(SEMS["v_ln"], 1)
            inc("v_ln")

            @on("scalar")
            def _(e, u=u, need=CN["v_ln"]):
                e.wait_ge(SEMS["v_ln"], need)
                e.activation(out=rstd_sb[u % 2], in_=mv_sb[u % 2][:, 1:2],
                             func=AF.Ln, bias=eps_sb)
                e.drain()
                e.activation(out=rstd_sb[u % 2], in_=rstd_sb[u % 2], func=AF.Exp,
                             scale=-0.5).then_inc(SEMS["c_ln"], 1)
            inc("c_ln")

            @on("vector")
            def _(e, u=u, buf=xt32[u % 2], outb=xn_tmb[u % 2], need=CN["c_ln"],
                  ptok=prev_tok_xs[u % 2]):
                e.wait_ge(SEMS["c_ln"], need)
                if ptok is not None:
                    Lhw.wait(e, ptok)
                e.tensor_scalar(out=outb, in0=buf, scalar1=mv_sb[u % 2][:, 0:1],
                                scalar2=rstd_sb[u % 2], op0=ALU.subtract, op1=ALU.mult)
                e.drain()
                e.nop().then_inc(SEMS["v_ln"], 1)
            inc("v_ln")

            Lhw.epoch()
            pxs = Lhw.plan(xn_dram[u * 128:(u + 1) * 128, :], xn_tmb[u % 2])
            tok_xs = Lhw.close()

            @on("sync")
            def _(e, p=pxs, need=CN["v_ln"]):
                e.wait_ge(SEMS["v_ln"], need)
                p(e)
            prev_tok_xs[u % 2] = tok_xs
            tile_xs_toks.append(tok_xs)

            if sub < 3:
                continue

            Lhw.epoch()
            pts = [Lhw.plan(xnf[T % 2][:, kc * 512:(kc + 1) * 512],
                            xn_dram[T * 512:(T + 1) * 512, kc * 128:(kc + 1) * 128],
                            transpose=True) for kc in range(8)]
            tok_xnf = Lhw.close()

            @on("sync")
            def _(e, plan=pts, T=T, pjn=CN["m_pj"], rgn=CN["m_rg"],
                  xstoks=list(tile_xs_toks[-4:])):
                for tk in xstoks:
                    Lhw.wait(e, tk)
                if T >= 2:
                    e.wait_ge(SEMS["m_pj"], pjn - 24)
                    if T >= 6:
                        e.wait_ge(SEMS["m_rg"], rgn)
                for p in plan:
                    p(e)

            if stage == "A" and T == 0:
                Lsw.epoch()
                pxd = Lsw.plan(dbg["xnf"], xnf[0][:, 0:512])
                tok_xd = Lsw.close()

                @on("gpsimd")
                def _(e, p=pxd, tok=tok_xnf):
                    Lhw.wait(e, tok)
                    p(e)

            Gp = CN["m_pj"]

            @on("tensor")
            def _(e, T=T, tok=tok_xnf, Gp=Gp, tokw=tok_sww, rp=CN["v_rp"]):
                Lhw.wait(e, tok)
                Lsw.wait(e, tokw)
                if T >= 1:
                    pass
                xn_t = xnf[T % 2]
                for g in range(24):
                    pi, mc2 = divmod(g, 8)
                    wsb = (wkb, wvb, wqb)[pi]
                    G = Gp + g
                    if G >= 2:
                        e.wait_ge(SEMS["c_pj"], G - 1)
                    for kc in range(8):
                        ins = e.matmul(
                            pp[G % 2][:, :],
                            wsb[:, kc * 1024 + mc2 * 128: kc * 1024 + (mc2 + 1) * 128],
                            xn_t[:, kc * 512:(kc + 1) * 512],
                            start=(kc == 0), stop=(kc == 7))
                    ins.then_inc(SEMS["m_pj"], 1)
            inc("m_pj", 24)

            vtoks = []
            vplans = []
            for mc2 in range(8):
                Lhw.epoch()
                vplans.append(Lhw.plan(
                    vfm_dram[mc2 * 128:(mc2 + 1) * 128, T * 512:(T + 1) * 512],
                    vst2[mc2 % 2]))
                vtoks.append(Lhw.close())
            all_vtoks.append(vtoks)

            @on("scalar")
            def _(e, T=T, Gp=Gp, rp_prev=CN["v_rp"],
                  ptk=prev_tok_k[0], pvt=list(prev_vtoks), vtoks=vtoks):
                for g in range(24):
                    pi, mc2 = divmod(g, 8)
                    dst = (kst8, None, qst8)[pi]
                    G = Gp + g
                    e.wait_ge(SEMS["m_pj"], G + 1)
                    if T >= 1 and g == 0:
                        e.wait_ge(SEMS["v_rp"], rp_prev)
                        if ptk is not None:
                            Lhw.wait(e, ptk)
                    if pi == 1:
                        if mc2 >= 2:
                            Lhw.wait(e, vtoks[mc2 - 2])
                        elif pvt:
                            Lhw.wait(e, pvt[6 + mc2])
                        dst_ap = vst2[mc2 % 2]
                    else:
                        dst_ap = dst[:, mc2 * 512:(mc2 + 1) * 512]
                    e.activation(out=dst_ap, in_=pp[G % 2][:, :],
                                 func=AF.Copy).then_inc(SEMS["c_pj"], 1)
            inc("c_pj", 24)

            @on("sync")
            def _(e, Gp=Gp, vplans=vplans, cp0=CN["c_pj"] - 24):
                for mc2 in range(8):
                    e.wait_ge(SEMS["c_pj"], cp0 + 8 + mc2 + 1)
                    vplans[mc2](e)

            Jp = CN["m_pm"]
            pj_now = CN["c_pj"]

            @on("tensor")
            def _(e, Jp=Jp, pj_now=pj_now, pm_ev=CN["c_pm"]):
                Lsw.wait(e, tok_swc)
                for j in range(16):
                    isk = j < 8
                    mc2 = j % 8
                    src = kst8 if isk else qst8
                    gi = (0 if isk else 16) + mc2
                    e.wait_ge(SEMS["c_pj"], pj_now - 24 + gi + 1)
                    J = Jp + j
                    if J >= 2:
                        e.wait_ge(SEMS["c_pm"], J - 1)
                    e.matmul(pq[J % 2][:, :], pswap,
                             src[:, mc2 * 512:(mc2 + 1) * 512],
                             start=True, stop=True).then_inc(SEMS["m_pm"], 1)
            inc("m_pm", 16)

            Jp2 = CN["m_pm"]

            @on("scalar")
            def _(e, Jp=Jp, rp_prev=CN["v_rp"], T=T):
                for j in range(16):
                    isk = j < 8
                    mc2 = j % 8
                    dst = ksw8 if isk else qsw8
                    e.wait_ge(SEMS["m_pm"], Jp + j + 1)
                    if T >= 1 and j == 0:
                        e.wait_ge(SEMS["v_rp"], rp_prev)
                    if j == 8:
                        e.wait_ge(SEMS["v_rp"], rp_prev + 8)
                    e.activation(out=dst[:, mc2 * 512:(mc2 + 1) * 512],
                                 in_=pq[(Jp + j) % 2][:, :],
                                 func=AF.Copy).then_inc(SEMS["c_pm"], 1)
            inc("c_pm", 16)

            pm_now = CN["c_pm"]
            rp0 = CN["v_rp"]

            @on("vector")
            def _(e, T=T, pm_now=pm_now):
                Lhw.wait(e, tok_hwc)
                c1 = cs1_sb[:, T * 512:(T + 1) * 512]
                c2 = cs2_sb[:, T * 512:(T + 1) * 512]
                for j in range(16):
                    isk = j < 8
                    mc2 = j % 8
                    raw = (kst8 if isk else qst8)[:, mc2 * 512:(mc2 + 1) * 512]
                    sww_ = (ksw8 if isk else qsw8)[:, mc2 * 512:(mc2 + 1) * 512]
                    e.wait_ge(SEMS["c_pm"], pm_now - 16 + j + 1)
                    e.drain()
                    e.tensor_tensor(out=t1_b, in0=raw, in1=c1, op=ALU.mult)
                    e.tensor_tensor(out=t2_b, in0=sww_, in1=c2, op=ALU.mult)
                    e.drain()
                    if isk:
                        e.tensor_tensor(out=raw, in0=t1_b, in1=t2_b, op=ALU.add)
                    else:
                        if T >= 4:
                            dst = q_own[:, mc2 * S_OWN + (T - 4) * 512:
                                        mc2 * S_OWN + (T - 4) * 512 + 512]
                        else:
                            dst = qscr
                        e.scalar_tensor_tensor(
                            out=dst, in0=t1_b, scalar=1.0, in1=t2_b,
                            op0=ALU.mult, op1=ALU.add,
                            accum_out=qbarp[:, mc2 * 8 + T: mc2 * 8 + T + 1])
                    e.drain()
                    e.nop().then_inc(SEMS["v_rp"], 1)
            inc("v_rp", 16)

            Lhw.epoch()
            pk = Lhw.plan(
                kfm_dram.rearrange("(c p) t -> p c t", p=128)[:, :, T * 512:(T + 1) * 512],
                kst8.rearrange("p (c t) -> p c t", c=8))
            tok_k = Lhw.close()

            @on("sync")
            def _(e, pk=pk, rp_k=rp0 + 8):
                e.wait_ge(SEMS["v_rp"], rp_k)
                pk(e)
            prev_tok_k[0] = tok_k
            prev_vtoks = vtoks

            if T >= 4:
                Gr = CN["m_rg"]

                @on("tensor")
                def _(e, T=T, Gr=Gr, rg_ev=CN["c_rg"]):
                    xn_t = xnf[T % 2]
                    if Gr >= 2:
                        e.wait_ge(SEMS["c_rg"], Gr - 1)
                    for kc in range(8):
                        i1 = e.matmul(pav[0][0:8, :], wr_b[:, kc * 8:(kc + 1) * 8],
                                      xn_t[:, kc * 512:(kc + 1) * 512],
                                      start=(kc == 0), stop=(kc == 7))
                    i1.then_inc(SEMS["m_rg"], 1)
                    for kc in range(8):
                        i2 = e.matmul(pav[1][0:3, :], wg_b[:, kc * 3:(kc + 1) * 3],
                                      xn_t[:, kc * 512:(kc + 1) * 512],
                                      start=(kc == 0), stop=(kc == 7))
                    i2.then_inc(SEMS["m_rg"], 1)
                inc("m_rg", 2)

                @on("scalar")
                def _(e, base=CN["m_rg"] - 2, tr_prev=CN["m_tr"]):
                    Lhw.wait(e, tok_hwc)
                    e.wait_ge(SEMS["m_rg"], base + 1)
                    if tr_prev:
                        e.wait_ge(SEMS["m_tr"], tr_prev)
                    e.activation(out=rsc_fm, in_=pav[0][0:8, :], func=AF.Relu,
                                 bias=br_sb).then_inc(SEMS["c_rg"], 1)
                    e.wait_ge(SEMS["m_rg"], base + 2)
                    e.activation(out=glog_fm, in_=pav[1][0:3, :], func=AF.Identity,
                                 bias=bg_sb).then_inc(SEMS["c_rg"], 1)
                inc("c_rg", 2)

                Tr = CN["m_tr"]

                @on("tensor")
                def _(e, Tr=Tr, need=CN["c_rg"], tm_prev=CN["v_tm"]):
                    e.wait_ge(SEMS["c_rg"], need)
                    for i in range(8):
                        Tg = Tr + i
                        if Tg >= 2:
                            e.wait_ge(SEMS["v_tm"], Tg - 1)
                        src = rsc_fm if i < 4 else glog_fm
                        rows = 8 if i < 4 else 3
                        colr = i % 4
                        e.transpose(pav[2 + (i % 2)][0:128, 0:rows],
                                    src[0:rows, colr * 128:(colr + 1) * 128],
                                    id32[0:rows, 0:rows]).then_inc(SEMS["m_tr"], 1)
                inc("m_tr", 8)

                @on("vector")
                def _(e, Tr=Tr, T=T):
                    for i in range(8):
                        rows = 8 if i < 4 else 3
                        colr = i % 4
                        sub4 = (T - 4) * 4 + colr
                        dst = (sc_tm[:, sub4 * 8:sub4 * 8 + 8] if i < 4 else
                               gl_tm[:, sub4 * 3:sub4 * 3 + 3])
                        e.wait_ge(SEMS["m_tr"], Tr + i + 1)
                        e.tensor_copy(dst, pav[2 + (i % 2)][0:128, 0:rows]
                                      ).then_inc(SEMS["v_tm"], 1)
                    e.drain()
                inc("v_tm", 8)


        A16.reset("A"); A32.reset("A")
        pe_b = A16.alloc(1024, "pe", rows=16)
        cpe_k = A16.alloc(1024, "cpek", rows=1)
        cpe_v = A16.alloc(1024, "cpev", rows=1)
        ktm_t = [A16.alloc(1024, f"ktm{i}") for i in range(2)]
        vtm_t = [A16.alloc(1024, f"vtm{i}") for i in range(2)]
        ck_st = A16.alloc(1024, "ck_st", rows=8)
        cv_st = A16.alloc(1024, "cv_st", rows=8)
        selk32 = A32.alloc(1024, "selk32", rows=16)

        ones_f = A16.alloc(1040, "ones_f")

        @on("vector")
        def _(e, need=CN["v_tm"]):
            e.wait_ge(SEMS["v_tm"], need)
            e.memset(ones_f, 1.0)
            e.drain()
            e.nop().then_inc(SEMS["v_cc"], 1)
        inc("v_cc")

        Lsw.epoch()
        p_ones = [Lsw.plan(pe_b, pe_d)]
        for t in range(32):
            p_ones.append(Lsw.plan(vtm_dram[t * 128:(t + 1) * 128, :], ones_f))
        p_ones.append(Lsw.plan(cv_dram[0:128, :], ones_f))
        p_ones.append(Lsw.plan(cv_dram[128:256, :], ones_f))
        tok_ones = Lsw.close()

        @on("gpsimd")
        def _(e, plan=p_ones, need=CN["v_cc"]):
            e.wait_ge(SEMS["v_cc"], need)
            for p in plan:
                p(e)

        Mc = CN["m_cc"]

        @on("tensor")
        def _(e, tokc=tok_swc, toko=tok_ones, rgn=CN["c_rg"], tmn=CN["v_tm"]):
            Lsw.wait(e, tokc)
            Lsw.wait(e, toko)
            e.wait_ge(SEMS["c_rg"], rgn)
            e.wait_ge(SEMS["v_tm"], tmn)
            for half in range(2):
                e.matmul(pav[half][0:1, 0:512], wkcol_b, pe_b[:, half * 512:(half + 1) * 512],
                         start=True, stop=True).then_inc(SEMS["m_cc"], 1)
                e.matmul(pav[2 + half][0:1, 0:512], wvcol_b, pe_b[:, half * 512:(half + 1) * 512],
                         start=True, stop=True).then_inc(SEMS["m_cc"], 1)
        inc("m_cc", 4)

        @on("scalar")
        def _(e, Mc=Mc):
            for half in range(2):
                e.wait_ge(SEMS["m_cc"], Mc + half * 2 + 1)
                e.activation(out=cpe_k[0:1, half * 512:(half + 1) * 512],
                             in_=pav[half][0:1, 0:512], func=AF.Copy).then_inc(SEMS["c_cc"], 1)
                e.wait_ge(SEMS["m_cc"], Mc + half * 2 + 2)
                e.activation(out=cpe_v[0:1, half * 512:(half + 1) * 512],
                             in_=pav[2 + half][0:1, 0:512], func=AF.Copy).then_inc(SEMS["c_cc"], 1)
        inc("c_cc", 4)

        ck_tok = []
        prev_tok_cc = [None]
        Mc0 = CN["m_cc"]
        Cc0 = CN["c_cc"]
        Vc0 = CN["v_cc"]
        for t in range(32):
            Lhw.epoch()
            pkt = Lhw.plan(ktm_t[t % 2], kfm_dram[:, t * 128:(t + 1) * 128], transpose=True)
            pvt = Lhw.plan(vtm_t[t % 2], vfm_dram[:, t * 128:(t + 1) * 128], transpose=True)
            tok_ld = Lhw.close()

            @on("sync")
            def _(e, t=t, pkt=pkt, pvt=pvt, ptk=prev_tok_k[0],
                  pvl=list(all_vtoks[-1]) if all_vtoks else [], mcn=CN["m_cc"]):
                if t == 0:
                    Lhw.wait(e, ptk)
                    for tk in pvl:
                        Lhw.wait(e, tk)
                if t >= 2:
                    e.wait_ge(SEMS["m_cc"], mcn - 4)
                pkt(e)
                pvt(e)

            Mct = CN["m_cc"]

            @on("tensor")
            def _(e, t=t, tok=tok_ld, Mct=Mct, ccn=CN["c_cc"]):
                Lhw.wait(e, tok)
                e.wait_ge(SEMS["c_cc"], ccn)
                for half in range(2):
                    e.matmul(pav[half][0:8, 0:512], wck_b,
                             ktm_t[t % 2][:, half * 512:(half + 1) * 512],
                             start=True, stop=False)
                    e.matmul(pav[half][0:8, 0:512], onesrow[0:1, 0:8],
                             cpe_k[0:1, half * 512:(half + 1) * 512],
                             start=False, stop=True).then_inc(SEMS["m_cc"], 1)
                    e.matmul(pav[2 + half][0:8, 0:512], wcv_b,
                             vtm_t[t % 2][:, half * 512:(half + 1) * 512],
                             start=True, stop=False)
                    e.matmul(pav[2 + half][0:8, 0:512], onesrow[0:1, 0:8],
                             cpe_v[0:1, half * 512:(half + 1) * 512],
                             start=False, stop=True).then_inc(SEMS["m_cc"], 1)
            inc("m_cc", 4)

            @on("scalar")
            def _(e, Mct=Mct, ptk=prev_tok_cc[0]):
                if ptk is not None:
                    Lhw.wait(e, ptk)
                for half in range(2):
                    e.wait_ge(SEMS["m_cc"], Mct + half * 2 + 1)
                    e.activation(out=ck_st[:, half * 512:(half + 1) * 512],
                                 in_=pav[half][0:8, 0:512],
                                 func=AF.Copy).then_inc(SEMS["c_cc"], 1)
                    e.wait_ge(SEMS["m_cc"], Mct + half * 2 + 2)
                    e.activation(out=cv_st[:, half * 512:(half + 1) * 512],
                                 in_=pav[2 + half][0:8, 0:512],
                                 func=AF.Copy).then_inc(SEMS["c_cc"], 1)
            inc("c_cc", 4)

            Lhw.epoch()
            pst = [
                Lhw.plan(ck_dram[8 * t:8 * t + 8, :], ck_st),
                Lhw.plan(cv_dram[8 * t:8 * t + 8, :].rearrange(
                    "r (h c) -> r h c", c=65)[:, :, 0:64],
                    cv_st.rearrange("r (h c) -> r h c", c=64)),
                Lhw.plan(ktm_dram[t * 128:(t + 1) * 128, :], ktm_t[t % 2]),
                Lhw.plan(vtm_dram[t * 128:(t + 1) * 128, :].rearrange(
                    "r (h c) -> r h c", c=65)[:, :, 0:64],
                    vtm_t[t % 2].rearrange("r (h c) -> r h c", c=64)),
            ]
            tok_st = Lhw.close()
            ck_tok.append(tok_st)
            prev_tok_cc[0] = tok_st

            @on("sync")
            def _(e, plan=pst, cc=CN["c_cc"], t=t, tko=tok_ones):
                if t == 0:
                    Lsw.wait(e, tko)
                e.wait_ge(SEMS["c_cc"], cc)
                for p in plan:
                    p(e)

        snap_col = hw_snap()
        Lhw.epoch()
        pcol = [Lhw.plan(cv_ext[:, 0:1040], cv_dram[0:128, :]),
                Lhw.plan(cv_ext[:, 1040:2080], cv_dram[128:256, :])]
        for kc in range(8):
            pcol.append(Lhw.plan(ck_fm[:, kc * NB:(kc + 1) * NB],
                                 ck_dram[:, kc * 128:(kc + 1) * 128], transpose=True))
        tok_col = Lhw.close()

        @on("sync")
        def _(e, plan=pcol, snap=snap_col):
            emit_fence(e, Lhw.sems, snap)
            for p in plan:
                p(e)

        @on("vector")
        def _(e, rp=CN["v_rp"]):
            e.wait_ge(SEMS["v_rp"], rp)
            for kc in range(8):
                e.tensor_reduce(out=qbar_sb[:, kc:kc + 1],
                                in_=qbarp[:, kc * 8:(kc + 1) * 8],
                                axis=AX.X, op=ALU.add)
            e.drain()
            e.tensor_copy(qbar_bf, qbar_sb)
            e.drain()
            e.nop().then_inc(SEMS["v_cc"], 1)
        inc("v_cc")

        @on("tensor")
        def _(e, vcc=CN["v_cc"], tok=tok_col, ccn=CN["c_cc"], vcn=CN["v_cc"]):
            Lhw.wait(e, tok)
            e.wait_ge(SEMS["v_cc"], vcc)
            e.wait_ge(SEMS["c_cc"], ccn)
            for kc in range(8):
                ins = e.matmul(pav[2][0:1, 0:NB], qbar_bf[:, kc:kc + 1],
                               ck_fm[:, kc * NB:(kc + 1) * NB],
                               start=(kc == 0), stop=(kc == 7))
            ins.then_inc(SEMS["m_im"], 1)
        inc("m_im")

        @on("vector")
        def _(e, need=CN["m_im"]):
            Lhw.wait(e, tok_hwc)
            e.wait_ge(SEMS["m_im"], need)
            e.tensor_copy(imp_sb, pav[2][0:1, 0:NB])
            e.drain()
            e.tensor_reduce(out=s1a, in_=imp_sb, axis=AX.X, op=ALU.max)
            e.drain()
            e.tensor_scalar(out=scrA, in0=imp_sb, scalar1=s1a, scalar2=None,
                            op0=ALU.is_ge)
            e.drain()
            e.tensor_tensor(out=scrB, in0=scrA, in1=iota256_sb, op=ALU.mult)
            e.tensor_tensor(out=scrC, in0=scrA, in1=blkabs_sb, op=ALU.mult)
            e.drain()
            e.tensor_reduce(out=s1b, in_=scrB, axis=AX.X, op=ALU.add)
            e.tensor_reduce(out=s1c, in_=scrC, axis=AX.X, op=ALU.add)
            e.drain()
            e.tensor_scalar(out=s1b, in0=s1b, scalar1=16.0, scalar2=None, op0=ALU.mult)
            e.scalar_tensor_tensor(out=scrB, in0=scrA, scalar=-1e30, in1=imp_sb,
                                   op0=ALU.mult, op1=ALU.add)
            e.drain()
            e.tensor_reduce(out=s1a, in_=scrB, axis=AX.X, op=ALU.max)
            e.drain()
            e.tensor_scalar(out=scrA, in0=scrB, scalar1=s1a, scalar2=None, op0=ALU.is_ge)
            e.drain()
            e.tensor_tensor(out=scrB, in0=scrA, in1=iota256_sb, op=ALU.mult)
            e.tensor_tensor(out=scrC, in0=scrA, in1=blkabs_sb, op=ALU.mult)
            e.drain()
            e.tensor_reduce(out=s1d, in_=scrB, axis=AX.X, op=ALU.add)
            e.tensor_reduce(out=s1a, in_=scrC, axis=AX.X, op=ALU.add)
            e.drain()
            e.tensor_scalar(out=s1d, in0=s1d, scalar1=16.0, scalar2=None, op0=ALU.mult)
            e.drain()
            e.nop().then_inc(SEMS["v_cc"], 1)
        inc("v_cc")

        Mp = CN["m_ps"]

        @on("tensor")
        def _(e, vcc=CN["v_cc"]):
            Lhw.wait(e, tok_hwc)
            e.wait_ge(SEMS["v_cc"], vcc)
            e.matmul(pav[3][0:16, 0:1], selpat0_sb, s1b, start=True, stop=False)
            e.matmul(pav[3][0:16, 0:1], selpat1_sb, s1d, start=False,
                     stop=True).then_inc(SEMS["m_ps"], 1)
            e.matmul(pav[1][0:16, 0:1], selpat0_sb, s1c, start=True, stop=False)
            e.matmul(pav[1][0:16, 0:1], selpat1_sb, s1a, start=False,
                     stop=True).then_inc(SEMS["m_ps"], 1)
        inc("m_ps", 2)

        @on("vector")
        def _(e, Mp=Mp):
            Lhw.wait(e, tok_hwc)
            e.wait_ge(SEMS["m_ps"], Mp + 2)
            e.tensor_copy(posperm, pav[3][0:16, 0:1])
            e.tensor_copy(posabs, pav[1][0:16, 0:1])
            e.drain()
            e.tensor_tensor(out=posperm, in0=posperm, in1=iota8_sb, op=ALU.add)
            e.tensor_tensor(out=posabs, in0=posabs, in1=iota8_sb, op=ALU.add)
            e.drain()
            e.tensor_copy(posu, posperm)
            e.drain()
            e.nop().then_inc(SEMS["v_cc"], 1)
        inc("v_cc")

        Lsw.epoch()
        pg = [Lsw.plan_indirect(selk_tm, ktm_dram, posu[:, 0:1]),
              Lsw.plan_indirect(selv_ext, vtm_dram, posu[:, 0:1])]
        tok_gather = Lsw.close()

        @on("gpsimd")
        def _(e, plan=pg, vcc=CN["v_cc"], snap=hw_snap(), tko=tok_ones):
            e.wait_ge(SEMS["v_cc"], vcc)
            emit_fence(e, Lhw.sems, snap)
            Lsw.wait(e, tko)
            for p in plan:
                p(e)

        @on("vector")
        def _(e, tok=tok_gather):
            Lsw.wait(e, tok)
            e.tensor_copy(selk32, selk_tm)
            e.drain()
            e.nop().then_inc(SEMS["v_cc"], 1)
        inc("v_cc")

        Tr3 = CN["m_tr"]

        @on("tensor")
        def _(e, vcc=CN["v_cc"], Tr3=Tr3, vcb=CN["v_cc"]):
            e.wait_ge(SEMS["v_cc"], vcc)
            for kc in range(8):
                if kc >= 2:
                    e.wait_ge(SEMS["v_cc"], vcb + (kc - 2) + 1)
                e.transpose(pav[kc % 2][0:128, 0:16],
                            selk32[:, kc * 128:(kc + 1) * 128],
                            id32[0:16, 0:16]).then_inc(SEMS["m_tr"], 1)
        inc("m_tr", 8)

        @on("vector")
        def _(e, Tr3=Tr3):
            for kc in range(8):
                e.wait_ge(SEMS["m_tr"], Tr3 + kc + 1)
                e.tensor_copy(selk_fm[:, kc * 16:(kc + 1) * 16],
                              pav[kc % 2][0:128, 0:16]).then_inc(SEMS["v_cc"], 1)
            e.drain()
        inc("v_cc", 8)


        for s16 in range(16):
            st_sl = sc_tm[:, s16 * 8:(s16 + 1) * 8]
            gl_sl = gl_tm[:, s16 * 3:(s16 + 1) * 3]

            @on("vector")
            def _(e, st_sl=st_sl, cm=CN["c_ms"], first=(s16 == 0), vt=CN["v_tm"]):
                if first:
                    Lhw.wait(e, tok_hwc)
                    e.wait_ge(SEMS["v_tm"], vt)
                if cm:
                    e.wait_ge(SEMS["c_ms"], cm)
                e.tensor_tensor(out=k8a, in0=st_sl, in1=delta8_sb, op=ALU.subtract)
                e.drain()
                e.tensor_reduce(out=m8[:, 0:1], in_=k8a, axis=AX.X, op=ALU.max)
                e.drain()
                e.tensor_scalar(out=k8b, in0=k8a, scalar1=m8[:, 0:1], scalar2=None,
                                op0=ALU.is_ge)
                e.drain()
                e.scalar_tensor_tensor(out=k8c, in0=k8b, scalar=-1e30, in1=k8a,
                                       op0=ALU.mult, op1=ALU.add)
                e.drain()
                e.tensor_reduce(out=m8[:, 1:2], in_=k8c, axis=AX.X, op=ALU.max)
                e.drain()
                e.tensor_scalar(out=k8c, in0=k8c, scalar1=m8[:, 1:2], scalar2=None,
                                op0=ALU.is_ge)
                e.drain()
                e.tensor_tensor(out=k8b, in0=k8b, in1=k8c, op=ALU.add)
                e.drain()
                e.tensor_scalar(out=k8c, in0=k8b, scalar1=1e30, scalar2=-1e30,
                                op0=ALU.mult, op1=ALU.add)
                e.drain()
                e.tensor_tensor(out=k8c, in0=k8c, in1=st_sl, op=ALU.add)
                e.drain()
                e.tensor_reduce(out=m8[:, 2:3], in_=k8c, axis=AX.X, op=ALU.max)
                e.drain()
                e.tensor_scalar(out=m8[:, 3:4], in0=m8[:, 2:3], scalar1=-1.0,
                                scalar2=None, op0=ALU.mult)
                e.drain()
                e.nop().then_inc(SEMS["v_tm"], 1)
            inc("v_tm")

            @on("scalar")
            def _(e, need=CN["v_tm"]):
                e.wait_ge(SEMS["v_tm"], need)
                e.activation(out=k8a, in_=k8c, func=AF.Exp,
                             bias=m8[:, 3:4]).then_inc(SEMS["c_ms"], 1)
            inc("c_ms")

            @on("vector")
            def _(e, need=CN["c_ms"]):
                e.wait_ge(SEMS["c_ms"], need)
                e.tensor_tensor(out=k8a, in0=k8a, in1=k8b, op=ALU.mult)
                e.drain()
                e.tensor_reduce(out=m8[:, 4:5], in_=k8a, axis=AX.X, op=ALU.add)
                e.drain()
                e.nop().then_inc(SEMS["v_tm"], 1)
            inc("v_tm")

            @on("scalar")
            def _(e, need=CN["v_tm"]):
                e.wait_ge(SEMS["v_tm"], need)
                e.activation(out=m8[:, 5:6], in_=m8[:, 4:5], func=AF.Ln)
                e.drain()
                e.activation(out=m8[:, 5:6], in_=m8[:, 5:6], func=AF.Exp,
                             scale=-1.0).then_inc(SEMS["c_ms"], 1)
            inc("c_ms")

            @on("vector")
            def _(e, s16=s16, st_sl=st_sl, gl_sl=gl_sl, need=CN["c_ms"]):
                e.wait_ge(SEMS["c_ms"], need)
                e.tensor_scalar(out=rw_tm[:, s16 * 8:(s16 + 1) * 8], in0=k8a,
                                scalar1=m8[:, 5:6], scalar2=None, op0=ALU.mult)
                e.tensor_reduce(out=m8[:, 6:7], in_=gl_sl, axis=AX.X, op=ALU.max)
                e.drain()
                e.tensor_scalar(out=m8[:, 7:8], in0=m8[:, 6:7], scalar1=-1.0,
                                scalar2=None, op0=ALU.mult)
                e.drain()
                e.nop().then_inc(SEMS["v_tm"], 1)
            inc("v_tm")

            @on("scalar")
            def _(e, gl_sl=gl_sl, need=CN["v_tm"]):
                e.wait_ge(SEMS["v_tm"], need)
                e.activation(out=k8b[:, 0:3], in_=gl_sl, func=AF.Exp,
                             bias=m8[:, 7:8]).then_inc(SEMS["c_ms"], 1)
            inc("c_ms")

            @on("vector")
            def _(e, need=CN["c_ms"]):
                e.wait_ge(SEMS["c_ms"], need)
                e.tensor_reduce(out=m8[:, 4:5], in_=k8b[:, 0:3], axis=AX.X, op=ALU.add)
                e.drain()
                e.nop().then_inc(SEMS["v_tm"], 1)
            inc("v_tm")

            @on("scalar")
            def _(e, need=CN["v_tm"]):
                e.wait_ge(SEMS["v_tm"], need)
                e.activation(out=m8[:, 5:6], in_=m8[:, 4:5], func=AF.Ln)
                e.drain()
                e.activation(out=m8[:, 5:6], in_=m8[:, 5:6], func=AF.Exp,
                             scale=-1.0).then_inc(SEMS["c_ms"], 1)
            inc("c_ms")

            @on("vector")
            def _(e, s16=s16, need=CN["c_ms"]):
                e.wait_ge(SEMS["c_ms"], need)
                e.tensor_scalar(out=g_tm[:, s16 * 3:(s16 + 1) * 3], in0=k8b[:, 0:3],
                                scalar1=m8[:, 5:6], scalar2=None, op0=ALU.mult)
                e.drain()
                e.nop().then_inc(SEMS["v_tm"], 1)
            inc("v_tm")



        if stage == "A":
            Lhw.epoch()
            pdbg = [Lhw.plan(dbg["imp"], imp_sb),
                    Lhw.plan(dbg["pos"][0:1, :].rearrange("a b -> b a"), posperm),
                    Lhw.plan(dbg["pos"][1:2, :].rearrange("a b -> b a"), posabs),
                    Lhw.plan(dbg["qbar"], qbar_sb),
                    Lhw.plan(dbg["rw"], rw_tm),
                    Lhw.plan(dbg["g3"], g_tm)]
            tok_dbg = Lhw.close()
            Lsw.epoch()
            psel = [Lsw.plan(dbg["selk"], selk_tm),
                    Lsw.plan(dbg["wvb"], wvb[:, 0:1024])]
            tok_selk = Lsw.close()

            @on("gpsimd")
            def _(e, plan=psel, vt=CN["v_tm"], vc=CN["v_cc"]):
                e.wait_ge(SEMS["v_tm"], vt)
                e.wait_ge(SEMS["v_cc"], vc)
                for p in plan:
                    p(e)

            @on("sync")
            def _(e, plan=pdbg, vt=CN["v_tm"], vc=CN["v_cc"],
                  hsnap=hw_snap(), ssnap=sw_snap()):
                e.wait_ge(SEMS["v_tm"], vt)
                e.wait_ge(SEMS["v_cc"], vc)
                for p in plan:
                    p(e)
                emit_fence(e, Lhw.sems, [c * 16 for c in Lhw.counts])
                emit_fence(e, Lsw.sems, [c * 16 for c in Lsw.counts])

            S.emit(blk)
            return nc

        A16.reset("A"); A32.reset("A")
        wo_c = A16.alloc(8 * 1024, "wo_c")
        krol = A16.alloc(8 * 768, "krol")
        vrol = A16.alloc(6 * 1040, "vrol")
        mw_sb = A16.alloc(6 * 512, "mw")
        mc_sb = A16.alloc(2 * 512, "mc")
        ms_sb = A16.alloc(512, "ms", rows=16)
        e_c = A16.alloc(2 * 512, "e_c")
        e_w = A16.alloc(6 * 512, "e_w")
        e_s = A16.alloc(512, "e_s", rows=16)
        av_sb = A16.alloc(512, "av_sb", rows=64)
        cprod = A16.alloc(512, "cprod")
        acc_qb = A16.alloc(8 * 512, "acc_qb")
        s_st8 = A16.alloc(8 * 512, "s_st8")
        grows_c = [A16.alloc(S_OWN, f"growc{b}", rows=1) for b in range(3)]

        Lsw.epoch()
        pwo = [Lsw.plan(wo_c[:, kc * 1024:(kc + 1) * 1024], wo_d[kc * 128:(kc + 1) * 128, :])
               for kc in range(8)]
        tok_wo = Lsw.close()

        @on("gpsimd")
        def _(e, plan=pwo):
            for p in plan:
                p(e)

        Tg0 = CN["m_tr"]
        vt_g = CN["v_tm"]

        @on("tensor")
        def _(e, Tg0=Tg0, vt_g=vt_g):
            e.wait_ge(SEMS["v_tm"], vt_g)
            for i in range(48):
                s16, b = divmod(i, 3)
                if i >= 2:
                    e.wait_ge(SEMS["v_tm"], vt_g + (i - 2) + 1)
                e.transpose(pav[i % 2][0:1, 0:128],
                            g_tm[:, s16 * 3 + b: s16 * 3 + b + 1],
                            id32).then_inc(SEMS["m_tr"], 1)
        inc("m_tr", 48)

        @on("vector")
        def _(e, Tg0=Tg0):
            for i in range(48):
                s16, b = divmod(i, 3)
                e.wait_ge(SEMS["m_tr"], Tg0 + i + 1)
                e.tensor_copy(grows_c[b][0:1, s16 * 128:(s16 + 1) * 128],
                              pav[i % 2][0:1, 0:128]).then_inc(SEMS["v_tm"], 1)
            e.drain()
            e.nop().then_inc(SEMS["v_tm"], 1)
        inc("v_tm", 49)

        prev_tok_s = [None]
        for qb in range(4):
            q0c = 2048 + qb * 512
            Lhw.epoch()
            pqb = [Lhw.plan(mw_sb.rearrange("p (c q) -> p c q", q=512),
                            mw_d[qb].rearrange("c p q -> p c q")),
                   Lhw.plan(mc_sb.rearrange("p (c q) -> p c q", q=512),
                            mc_d[qb].rearrange("c p q -> p c q"))]
            for kc in range(8):
                pqb.append(Lhw.plan(krol[:, kc * 768:(kc + 1) * 768],
                                    kfm_dram[kc * 128:(kc + 1) * 128,
                                             q0c - 256:q0c + 512]))
            for w in range(6):
                pqb.append(Lhw.plan(vrol[:, w * 1040:(w + 1) * 1040],
                                    vtm_dram[q0c - 256 + w * 128:
                                             q0c - 256 + (w + 1) * 128, :]))
            tok_qb = Lhw.close()

            @on("sync")
            def _(e, plan=pqb, qb=qb, msn=CN["m_sc"], man=CN["m_av"]):
                if qb > 0:
                    e.wait_ge(SEMS["m_sc"], msn)
                    e.wait_ge(SEMS["m_av"], man)
                for p in plan:
                    p(e)

            @on("vector")
            def _(e, qb=qb, vc=CN["v_cc"], man=CN["m_av"]):
                Lhw.wait(e, tok_hwc)
                e.wait_ge(SEMS["v_cc"], vc)
                if qb > 0:
                    e.wait_ge(SEMS["m_av"], man)
                e.tensor_scalar(out=ms_sb, in0=qpos_sb[:, qb * 512:(qb + 1) * 512],
                                scalar1=posabs, scalar2=None, op0=ALU.is_ge)
                e.drain()
                e.nop().then_inc(SEMS["v_ep"], 1)
            inc("v_ep")

            for h in range(16):
                hr = (h % 2) * 64
                kc_h = h // 2
                qsl = q_own[hr:hr + 64, kc_h * S_OWN + qb * 512: kc_h * S_OWN + qb * 512 + 512]
                Gs = CN["m_sc"]
                chunks = []
                for c in range(2):
                    chunks.append(("c", c, ck_fm[hr:hr + 64, kc_h * NB + c * 128: kc_h * NB + (c + 1) * 128], 128))
                for w in range(6):
                    chunks.append(("w", w, krol[hr:hr + 64, kc_h * 768 + w * 128: kc_h * 768 + (w + 1) * 128], 128))
                chunks.append(("s", 0, selk_fm[hr:hr + 64, kc_h * 16: kc_h * 16 + 16], 16))

                @on("tensor")
                def _(e, chunks=chunks, qsl=qsl, Gs=Gs, tok=tok_qb, tokw=tok_gather,
                      vcc=CN["v_cc"], h=h, qb=qb, epn=CN["v_ep"], can=CN["c_av"]):
                    Lhw.wait(e, tok)
                    if h == 0 and qb == 0:
                        Lsw.wait(e, tokw)
                        e.wait_ge(SEMS["v_cc"], vcc)
                    for i, (fam, ci, lh, rows) in enumerate(chunks):
                        G = Gs + i
                        if G >= 2:
                            e.wait_ge(SEMS["c_ex"], G - 1)
                        e.matmul(pp[G % 2][0:rows, :], lh, qsl,
                                 start=True, stop=True).then_inc(SEMS["m_sc"], 1)
                inc("m_sc", 9)

                Cx = CN["c_ex"]

                @on("scalar")
                def _(e, chunks=chunks, Gs=Gs, man=CN["m_av"], h=h):
                    for i, (fam, ci, lh, rows) in enumerate(chunks):
                        e.wait_ge(SEMS["m_sc"], Gs + i + 1)
                        if (h > 0 or qb > 0) and i == 0:
                            e.wait_ge(SEMS["m_av"], man)
                        dst = (e_c[:, ci * 512:(ci + 1) * 512] if fam == "c" else
                               e_w[:, ci * 512:(ci + 1) * 512] if fam == "w" else e_s)
                        e.activation(out=dst, in_=pp[(Gs + i) % 2][0:rows, :],
                                     func=AF.Exp, bias=negC_sb[0:rows, :],
                                     scale=SCALE).then_inc(SEMS["c_ex"], 1)
                inc("c_ex", 9)

                Vm = CN["v_ep"]

                @on("vector")
                def _(e, chunks=chunks, Cx=Cx, qb=qb):
                    e.drain()
                    for i, (fam, ci, lh, rows) in enumerate(chunks):
                        e.wait_ge(SEMS["c_ex"], Cx + i + 1)
                        if fam == "c":
                            e.tensor_tensor(out=e_c[:, ci * 512:(ci + 1) * 512],
                                            in0=e_c[:, ci * 512:(ci + 1) * 512],
                                            in1=mc_sb[:, ci * 512:(ci + 1) * 512],
                                            op=ALU.mult)
                        elif fam == "w":
                            e.tensor_tensor(out=e_w[:, ci * 512:(ci + 1) * 512],
                                            in0=e_w[:, ci * 512:(ci + 1) * 512],
                                            in1=mw_sb[:, ci * 512:(ci + 1) * 512],
                                            op=ALU.mult)
                        else:
                            e.tensor_tensor(out=e_s, in0=e_s, in1=ms_sb, op=ALU.mult)
                        e.drain()
                        e.nop().then_inc(SEMS["v_ep"], 1)
                inc("v_ep", 9)

                Ga = CN["m_av"]

                @on("tensor")
                def _(e, h=h, qb=qb, Vm=Vm, epn=CN["v_ep"], can=CN["c_av"]):
                    if h > 0 or qb > 0:
                        e.wait_ge(SEMS["v_ep"], Vm)
                        e.wait_ge(SEMS["c_av"], can)
                    for c in range(2):
                        e.wait_ge(SEMS["v_ep"], Vm + c + 1)
                        e.matmul(pav[0][0:65, :],
                                 cv_ext[:, c * 1040 + h * 65: c * 1040 + (h + 1) * 65],
                                 e_c[:, c * 512:(c + 1) * 512],
                                 start=(c == 0), stop=(c == 1)).then_inc(SEMS["m_av"], 1)
                    for w in range(6):
                        e.wait_ge(SEMS["v_ep"], Vm + 2 + w + 1)
                        e.matmul(pav[1][0:65, :],
                                 vrol[:, w * 1040 + h * 65: w * 1040 + (h + 1) * 65],
                                 e_w[:, w * 512:(w + 1) * 512],
                                 start=(w == 0), stop=(w == 5)).then_inc(SEMS["m_av"], 1)
                    e.wait_ge(SEMS["v_ep"], Vm + 9)
                    e.matmul(pav[2][0:65, :], selv_ext[0:16, h * 65:(h + 1) * 65],
                             e_s, start=True, stop=True).then_inc(SEMS["m_av"], 1)
                inc("m_av", 9)

                accsl = acc_qb[hr:hr + 64, kc_h * 512:(kc_h + 1) * 512]
                br_list = [(0, pav[0], 2), (1, pav[2], 9), (2, pav[1], 8)]
                Bc = CN["m_bc"]
                Cv = CN["c_av"]
                Ve = CN["v_ep"]
                @on("vector")
                def _(e, br_list=br_list, Ga=Ga, Bc=Bc, Cv=Cv, qb=qb, accsl=accsl,
                      h=h, hr=hr, won=CN["m_wo"]):
                    if qb > 0 and h == 0:
                        e.wait_ge(SEMS["m_wo"], won)
                    for bi, (gidx, bank, avdone) in enumerate(br_list):
                        e.wait_ge(SEMS["m_av"], Ga + avdone)
                        e.drain()
                        e.tensor_scalar(out=den_g, in0=bank[64:65, :], scalar1=1e-18,
                                        scalar2=None, op0=ALU.max)
                        e.drain()
                        e.nop().then_inc(SEMS["v_ep"], 1)
                        e.wait_ge(SEMS["c_av"], Cv + bi * 2 + 1)
                        e.tensor_tensor(out=gr_bf, in0=r_bf,
                                        in1=grows_c[gidx][0:1, qb * 512:(qb + 1) * 512],
                                        op=ALU.mult)
                        e.drain()
                        e.nop().then_inc(SEMS["v_ep"], 1)
                        e.wait_ge(SEMS["c_av"], Cv + bi * 2 + 2)
                        e.wait_ge(SEMS["m_bc"], Bc + bi + 1)
                        if bi == 0:
                            e.tensor_tensor(out=accsl, in0=av_sb, in1=pav[3][0:64, :],
                                            op=ALU.mult)
                        else:
                            cpr = cprod[hr:hr + 64, :] if hr else cprod[0:64, :]
                            e.tensor_tensor(out=cpr, in0=av_sb, in1=pav[3][0:64, :],
                                            op=ALU.mult)
                            e.drain()
                            e.tensor_tensor(out=accsl, in0=accsl, in1=cpr, op=ALU.add)
                        e.drain()
                        e.nop().then_inc(SEMS["v_ep"], 1)
                inc("v_ep", 9)
                inc("c_av", 6)

                @on("scalar")
                def _(e, br_list=br_list, Ve=Ve, Ga=Ga):
                    for bi, (gidx, bank, avdone) in enumerate(br_list):
                        e.wait_ge(SEMS["v_ep"], Ve + bi * 3 + 1)
                        e.activation(out=den_g, in_=den_g, func=AF.Ln)
                        e.drain()
                        e.activation(out=r_bf, in_=den_g, func=AF.Exp,
                                     scale=-1.0).then_inc(SEMS["c_av"], 1)
                        e.drain()
                        e.wait_ge(SEMS["m_av"], Ga + avdone)
                        e.activation(out=av_sb, in_=bank[0:64, :],
                                     func=AF.Copy).then_inc(SEMS["c_av"], 1)

                @on("tensor")
                def _(e, br_list=br_list, Ve=Ve, Bc=Bc):
                    for bi in range(3):
                        e.wait_ge(SEMS["v_ep"], Ve + bi * 3 + 2)
                        if Bc + bi >= 1:
                            e.wait_ge(SEMS["v_ep"], Ve + bi * 3)
                        e.matmul(pav[3][0:64, :], onesrow, gr_bf,
                                 start=True, stop=True).then_inc(SEMS["m_bc"], 1)
                inc("m_bc", 3)

            Gw = CN["m_wo"]

            @on("tensor")
            def _(e, qb=qb, tokw=tok_wo, epn=CN["v_ep"], can=CN["c_ms"]):
                Lsw.wait(e, tokw)
                e.wait_ge(SEMS["v_ep"], epn)
                for mo in range(8):
                    if mo >= 2:
                        e.wait_ge(SEMS["c_ms"], can + mo - 2 + 1)
                    for kc in range(8):
                        ins = e.matmul(pq[mo % 2][:, :],
                                       wo_c[:, kc * 1024 + mo * 128: kc * 1024 + (mo + 1) * 128],
                                       acc_qb[:, kc * 512:(kc + 1) * 512],
                                       start=(kc == 0), stop=(kc == 7))
                    ins.then_inc(SEMS["m_wo"], 1)
            inc("m_wo", 8)

            @on("scalar")
            def _(e, Gw=Gw, ptoks=prev_tok_s[0]):
                Lhw.wait(e, tok_hwc)
                if ptoks is not None:
                    Lhw.wait(e, ptoks)
                for mo in range(8):
                    e.wait_ge(SEMS["m_wo"], Gw + mo + 1)
                    e.activation(out=s_st8[:, mo * 512:(mo + 1) * 512],
                                 in_=pq[mo % 2][:, :], func=AF.Identity,
                                 bias=bo_sb[:, mo:mo + 1]).then_inc(SEMS["c_ms"], 1)
            inc("c_ms", 8)

            Lhw.epoch()
            ps_st = Lhw.plan(
                sfm_dram.rearrange("(c p) t -> p c t", p=128)[:, :, qb * 512:(qb + 1) * 512],
                s_st8.rearrange("p (c t) -> p c t", c=8))
            tok_s = Lhw.close()

            @on("sync")
            def _(e, p=ps_st, need=CN["c_ms"]):
                e.wait_ge(SEMS["c_ms"], need)
                p(e)
            prev_tok_s[0] = tok_s

        if stage == "C":
            @on("sync")
            def _(e, wv=CN["v_ep"], wc=CN["c_ms"]):
                e.wait_ge(SEMS["v_ep"], wv)
                e.wait_ge(SEMS["c_ms"], wc)
                emit_fence(e, Lhw.sems, [c * 16 for c in Lhw.counts])
                emit_fence(e, Lsw.sems, [c * 16 for c in Lsw.counts])
            S.emit(blk)
            return nc

        A16.reset("persist"); A32.reset("A")
        xnE = [A16.alloc(8 * 512, f"xnE{i}") for i in range(2)]
        w1sb = A16.alloc(8 * 2048, "w1sb")
        w2sb = A16.alloc(16 * 1024, "w2sb")
        hsb = A16.alloc(16 * 512, "hsb")
        accf = A16.alloc(16 * 1024, "accf")
        rwT = A16.alloc(S_OWN, "rwT", rows=8)
        b2sb = A16.alloc(1024, "b2sb", rows=8)

        EB = dict(m_wo=CN["m_wo"], c_ms=CN["c_ms"], v_ep=CN["v_ep"],
                  m_av=CN["m_av"], m_sc=CN["m_sc"])

        Lsw.epoch()
        pb2 = [Lsw.plan(b2sb, b2_d)]
        tok_b2 = Lsw.close()

        @on("gpsimd")
        def _(e, plan=pb2, EB=EB):
            e.wait_ge(SEMS["m_wo"], EB["m_wo"])
            for p in plan:
                p(e)

        TrE = CN["m_tr"]

        @on("tensor")
        def _(e, TrE=TrE, EB=EB, vff=CN["v_ff"]):
            e.wait_ge(SEMS["v_ep"], EB["v_ep"])
            e.wait_ge(SEMS["c_ms"], EB["c_ms"])
            for t in range(16):
                if t >= 2:
                    e.wait_ge(SEMS["v_ff"], vff + (t - 2) + 1)
                e.transpose(pav[t % 2][0:8, 0:128], rw_tm[:, t * 8:(t + 1) * 8],
                            id32).then_inc(SEMS["m_tr"], 1)
        inc("m_tr", 16)

        @on("vector")
        def _(e, TrE=TrE, EB=EB, ts_tok=prev_tok_s[0]):
            e.wait_ge(SEMS["v_ep"], EB["v_ep"])
            Lhw.wait(e, ts_tok)
            for t in range(16):
                e.wait_ge(SEMS["m_tr"], TrE + t + 1)
                e.tensor_copy(rwT[0:8, t * 128:(t + 1) * 128],
                              pav[t % 2][0:8, 0:128]).then_inc(SEMS["v_ff"], 1)
            e.drain()
            e.nop().then_inc(SEMS["v_ff"], 1)
        inc("v_ff", 17)

        R8 = CN["m_im"]

        @on("tensor")
        def _(e, R8=R8, vff=CN["v_ff"], tokb=tok_b2):
            e.wait_ge(SEMS["v_ff"], vff)
            Lsw.wait(e, tokb)
            for i in range(32):
                ts, dh = divmod(i, 2)
                if i >= 2:
                    e.wait_ge(SEMS["v_ff"], vff + (i - 2) + 1)
                e.matmul(pav[i % 2][0:128, 0:512], rwT[0:8, ts * 128:(ts + 1) * 128],
                         b2sb[:, dh * 512:(dh + 1) * 512],
                         start=True, stop=True).then_inc(SEMS["m_im"], 1)
        inc("m_im", 32)

        @on("vector")
        def _(e, R8=R8):
            for i in range(32):
                ts, dh = divmod(i, 2)
                e.wait_ge(SEMS["m_im"], R8 + i + 1)
                e.tensor_copy(accf[:, ts * 1024 + dh * 512: ts * 1024 + (dh + 1) * 512],
                              pav[i % 2][0:128, 0:512]).then_inc(SEMS["v_ff"], 1)
            e.drain()
            e.nop().then_inc(SEMS["v_ff"], 1)
        inc("v_ff", 33)

        for e_i in range(8):
            for hh in range(2):
                phase = e_i * 2 + hh
                need_f2 = CN["m_f2"]
                need_f1 = CN["m_f1"]
                Lsw.epoch()
                planw = [Lsw.plan(w1sb[:, kc * 2048:(kc + 1) * 2048],
                                  w1_d[e_i, kc * 128:(kc + 1) * 128,
                                       hh * 2048:(hh + 1) * 2048])
                         for kc in range(8)]
                planw += [Lsw.plan(w2sb[:, hc * 1024:(hc + 1) * 1024],
                                   w2_d[e_i, hh * 2048 + hc * 128:
                                        hh * 2048 + (hc + 1) * 128, :])
                          for hc in range(16)]
                tok_w = Lsw.close()

                @on("gpsimd")
                def _(e, phase=phase, planw=planw, f1=need_f1, f2=need_f2):
                    if phase > 0:
                        e.wait_ge(SEMS["m_f1"], f1)
                        e.wait_ge(SEMS["m_f2"], f2)
                    for p in planw:
                        p(e)

                for tt in range(4):
                    jglob = phase * 4 + tt
                    Lhw.epoch()
                    pxn = [Lhw.plan(xnE[jglob % 2][:, kc * 512:(kc + 1) * 512],
                                    xn_dram[2048 + tt * 512: 2048 + (tt + 1) * 512,
                                            kc * 128:(kc + 1) * 128], transpose=True)
                           for kc in range(8)]
                    tok_xnE = Lhw.close()

                    @on("sync")
                    def _(e, plan=pxn, jglob=jglob, f1n=CN["m_f1"], EB=EB,
                          ts_tok=prev_tok_s[0]):
                        if jglob == 0:
                            e.wait_ge(SEMS["m_wo"], EB["m_wo"])
                            e.wait_ge(SEMS["c_ms"], EB["c_ms"])
                            e.wait_ge(SEMS["v_ep"], EB["v_ep"])
                            Lhw.wait(e, ts_tok)
                        elif jglob >= 2:
                            e.wait_ge(SEMS["m_f1"], f1n - 16)
                        for p in plan:
                            p(e)

                    F1 = CN["m_f1"]

                    @on("tensor")
                    def _(e, tt=tt, tok=tok_w, tokx=tok_xnE, F1=F1, ch=CN["c_h"],
                          vff=CN["v_ff"]):
                        Lsw.wait(e, tok)
                        Lhw.wait(e, tokx)
                        e.wait_ge(SEMS["v_ff"], vff)
                        for m in range(16):
                            G = F1 + m
                            if G >= 2:
                                e.wait_ge(SEMS["c_h"], G - 1)
                            for kc in range(8):
                                ins = e.matmul(
                                    pp[G % 2][:, :],
                                    w1sb[:, kc * 2048 + m * 128: kc * 2048 + (m + 1) * 128],
                                    xnE[(phase * 4 + tt) % 2][:, kc * 512:(kc + 1) * 512],
                                    start=(kc == 0), stop=(kc == 7))
                            ins.then_inc(SEMS["m_f1"], 1)
                    inc("m_f1", 16)

                    @on("scalar")
                    def _(e, e_i=e_i, hh=hh, F1=F1):
                        Lhw.wait(e, tok_hwc)
                        for m in range(16):
                            e.wait_ge(SEMS["m_f1"], F1 + m + 1)
                            e.activation(out=hsb[:, m * 512:(m + 1) * 512],
                                         in_=pp[(F1 + m) % 2][:, :], func=AF.Relu,
                                         bias=b1_sb[:, e_i * 32 + hh * 16 + m:
                                                    e_i * 32 + hh * 16 + m + 1],
                                         scale=1.0).then_inc(SEMS["c_h"], 1)
                    inc("c_h", 16)

                    F2 = CN["m_f2"]

                    @on("tensor")
                    def _(e, tt=tt, ch=CN["c_h"], F2=F2, vff=CN["v_ff"]):
                        e.wait_ge(SEMS["c_h"], ch)
                        for g in range(8):
                            ts, dh = divmod(g, 2)
                            G2 = F2 + g
                            if G2 >= 2:
                                e.wait_ge(SEMS["v_ff"], vff + (g - 2) + 1 if g >= 2 else 0)
                            for m in range(16):
                                ins = e.matmul(
                                    pav[G2 % 2][0:128, 0:512],
                                    hsb[:, m * 512 + ts * 128: m * 512 + (ts + 1) * 128],
                                    w2sb[:, m * 1024 + dh * 512: m * 1024 + (dh + 1) * 512],
                                    start=(m == 0), stop=(m == 15))
                            ins.then_inc(SEMS["m_f2"], 1)
                    inc("m_f2", 8)

                    @on("vector")
                    def _(e, tt=tt, F2=F2, e_i=e_i):
                        for g in range(8):
                            ts, dh = divmod(g, 2)
                            gts = tt * 4 + ts
                            e.wait_ge(SEMS["m_f2"], F2 + g + 1)
                            e.drain()
                            e.scalar_tensor_tensor(
                                out=accf[:, gts * 1024 + dh * 512: gts * 1024 + (dh + 1) * 512],
                                in0=pav[(F2 + g) % 2][0:128, 0:512],
                                scalar=rw_tm[:, gts * 8 + e_i: gts * 8 + e_i + 1],
                                in1=accf[:, gts * 1024 + dh * 512: gts * 1024 + (dh + 1) * 512],
                                op0=ALU.mult, op1=ALU.add).then_inc(SEMS["v_ff"], 1)
                    inc("v_ff", 8)

        xf32 = A32.alloc(1024, "xf32")
        sil32 = A32.alloc(1024, "sil32")
        t32 = A32.alloc(1024, "t32")
        stm = A16.alloc(1024, "stm")

        prev_tok_o = [None]
        for st in range(16):
            Lhw.epoch()
            pfx = [Lhw.plan(xf32, x_in[2048 + st * 128: 2048 + (st + 1) * 128, :]),
                   Lhw.plan(stm, sfm_dram[:, st * 128:(st + 1) * 128], transpose=True)]
            tok_f = Lhw.close()

            @on("sync")
            def _(e, plan=pfx, st=st, vfin=CN["v_fin"], pto=prev_tok_o[0]):
                if st > 0:
                    e.wait_ge(SEMS["v_fin"], vfin)
                if pto is not None:
                    Lhw.wait(e, pto)
                for p in plan:
                    p(e)

            @on("scalar")
            def _(e, st=st, vff=CN["v_ff"], vfin=CN["v_fin"]):
                e.wait_ge(SEMS["v_ff"], vff)
                if st > 0:
                    e.wait_ge(SEMS["v_fin"], vfin)
                e.activation(out=sil32, in_=accf[:, st * 1024:(st + 1) * 1024],
                             func=AF.Silu).then_inc(SEMS["c_sl"], 1)
            inc("c_sl")

            @on("vector")
            def _(e, st=st, tok=tok_f, need=CN["c_sl"]):
                Lhw.wait(e, tok)
                e.wait_ge(SEMS["c_sl"], need)
                e.tensor_tensor(out=t32, in0=stm, in1=sil32, op=ALU.mult)
                e.drain()
                e.tensor_tensor(out=t32, in0=t32, in1=xf32, op=ALU.add)
                e.drain()
                e.nop().then_inc(SEMS["v_fin"], 1)
            inc("v_fin")

            Lhw.epoch()
            pout = Lhw.plan(x1_out[st * 128:(st + 1) * 128, :], t32)
            tok_o = Lhw.close()

            @on("sync")
            def _(e, p=pout, need=CN["v_fin"]):
                e.wait_ge(SEMS["v_fin"], need)
                p(e)
            prev_tok_o[0] = tok_o

        @on("sync")
        def _(e):
            emit_fence(e, Lhw.sems, [c * 16 for c in Lhw.counts])
            emit_fence(e, Lsw.sems, [c * 16 for c in Lsw.counts])

        S.emit(blk)
        return nc




def host_weights_l1(ins):
    f32 = np.float32
    wqkv = ins["L1_attn_W_qkv"].astype(f32)
    bqkv = ins["L1_attn_b_qkv"].astype(f32)
    b1 = ins["L1_ffn_b1"].astype(f32)
    b1ct = np.ascontiguousarray(b1.reshape(8, 32, 128).transpose(2, 0, 1).reshape(128, 256))
    return {
        "wq1": np.ascontiguousarray(wqkv[:, 0:1024]),
        "wk1": np.ascontiguousarray(wqkv[:, 1024:2048]),
        "wv1": np.ascontiguousarray(wqkv[:, 2048:3072]),
        "bq_ct": np.ascontiguousarray(bqkv[0:1024].reshape(8, 128).T),
        "bk_ct": np.ascontiguousarray(bqkv[1024:2048].reshape(8, 128).T),
        "bv_ct": np.ascontiguousarray(bqkv[2048:3072].reshape(8, 128).T),
        "wproj": ins["L1_attn_W_proj"].astype(f32),
        "bp_ct": np.ascontiguousarray(ins["L1_attn_b_proj"].astype(f32).reshape(8, 128).T),
        "lnw_bc": np.ascontiguousarray(
            np.broadcast_to(ins["L1_attn_ln_w"].astype(f32)[None, :], (128, 1024))),
        "lnb_bc": np.ascontiguousarray(
            np.broadcast_to(ins["L1_attn_ln_b"].astype(f32)[None, :], (128, 1024))),
        "gw": ins["L1_ffn_gate_w"].astype(f32),
        "gb8": ins["L1_ffn_gate_b"].astype(f32)[:, None].copy(),
        "w1": ins["L1_ffn_w1"].astype(f32), "b1ct": b1ct,
        "w2": ins["L1_ffn_w2"].astype(f32), "b2": ins["L1_ffn_b2"].astype(f32),
    }


def l1_in_map(ins, x1_full, be, hf):
    m = {"x1_perm": perm_x(x1_full[be].astype(np.float32), hf),
         "ident": np.eye(128, dtype=np.float32),
         "delta8": np.ascontiguousarray(
             np.broadcast_to((np.arange(8) * DELTA).astype(np.float32)[None, :], (128, 8))),
         "iota8c": np.ascontiguousarray(
             np.broadcast_to(np.arange(8, dtype=np.float32)[None, :], (128, 8)))}
    m.update(host_weights_l1(ins))
    return m


def build_l1(stage="F"):
    nc = bass.Bass("TRN2", target_bir_lowering=False)
    dram = lambda name, shape, dt=F32, kind="ExternalInput": \
        nc.dram_tensor(name, shape, dt, kind=kind).ap()

    x_in = dram("x1_perm", [S_FULL, D])
    wq_d = dram("wq1", [D, D]); wk_d = dram("wk1", [D, D]); wv_d = dram("wv1", [D, D])
    bq_d = dram("bq_ct", [128, 8]); bk_d = dram("bk_ct", [128, 8]); bv_d = dram("bv_ct", [128, 8])
    wp_d = dram("wproj", [D, D]); bp_d = dram("bp_ct", [128, 8])
    lnw_d = dram("lnw_bc", [128, D]); lnb_d = dram("lnb_bc", [128, D])
    gw_d = dram("gw", [D, 8]); gb_d = dram("gb8", [8, 1])
    w1_d = dram("w1", [8, D, HIDDEN]); b1_d = dram("b1ct", [128, 256])
    w2_d = dram("w2", [8, HIDDEN, D]); b2_d = dram("b2", [8, D])
    ident_d = dram("ident", [128, 128])
    delta8_d = dram("delta8", [128, 8])
    iota8c_d = dram("iota8c", [128, 8])

    x2_out = dram("x2", [S_OWN, D], F32, kind="ExternalOutput")
    scrkind = "ExternalOutput" if stage == "A" else "Internal"
    xn_dram = dram("xn1_scr", [S_FULL, D], BF16, kind=scrkind)
    zn_dram = dram("zn_scr", [S_FULL, D], BF16, kind=scrkind)
    kfm_dram = dram("kfm_scr", [D, S_FULL], BF16, kind=scrkind)
    vfm_dram = dram("vfm_scr", [D, S_FULL], BF16, kind="Internal")
    vtm_dram = dram("vtm_scr", [S_FULL, 1040], BF16, kind=scrkind)
    qfm_dram = dram("qfm_scr", [D, S_OWN], BF16, kind="Internal")
    sfm_dram = dram("sfm_scr", [D, S_OWN], BF16,
                    kind="ExternalOutput" if stage == "C" else "Internal")
    xs_dram = dram("xs_scr", [S_OWN, D], BF16, kind="Internal")
    dbg = {}
    if stage == "A":
        dbg["wisel"] = dram("d_wisel", [128, 128], F32, kind="ExternalOutput")
        dbg["selw"] = dram("d_selw", [128, 128], F32, kind="ExternalOutput")

    with ExitStack() as stack:
        ec = stack.enter_context
        A16 = Arena(ec(nc.sbuf_tensor("arena16", [128, 88800], BF16)).ap(), 88800, "bf")
        A32 = Arena(ec(nc.sbuf_tensor("arena32", [128, 7300], F32)).ap(), 7300, "f32")

        idbf = A16.alloc(128, "ident")
        onesrow = A16.alloc(64, "ones1x64", rows=1)
        r_bf = A16.alloc(512, "r_bf", rows=1)
        b2sb = A16.alloc(1024, "b2sb", rows=8)
        selT = A16.alloc(S_OWN, "selT", rows=8)

        id32 = A32.alloc(128, "id32")
        bq_sb = A32.alloc(8, "bq"); bk_sb = A32.alloc(8, "bk"); bv_sb = A32.alloc(8, "bv")
        bp_sb = A32.alloc(8, "bp"); b1_sb = A32.alloc(256, "b1")
        gb_sb = A32.alloc(1, "gb", rows=8)
        delta8_sb = A32.alloc(8, "delta8")
        iota8c_sb = A32.alloc(8, "iota8c")
        gl_tm = A32.alloc(128, "gl_tm")
        wisel_tm = A32.alloc(128, "wisel_tm")
        selw_tm = A32.alloc(128, "selw_tm")
        mvst = [A32.alloc(12, f"st{i}") for i in range(2)]
        mv_sb = [A32.alloc(2, f"mv{i}") for i in range(2)]
        rstd_sb = [A32.alloc(2, f"rs{i}") for i in range(2)]
        eps_sb = A32.alloc(1, "eps")
        negC_sb = A32.alloc(1, "negC")
        den_g = A32.alloc(512, "deng", rows=1)
        m8 = A32.alloc(8, "m8")
        k8a = A32.alloc(8, "k8a"); k8b = A32.alloc(8, "k8b"); k8c = A32.alloc(8, "k8c")
        xf32 = A32.alloc(1024, "xf32")
        sil32 = A32.alloc(1024, "sil32")
        t32 = A32.alloc(1024, "t32")
        A32.mark("base")

        A16.mark("persist")

        pp = [ec(nc.psum_tensor(f"pp{i}", [128, 512], F32)) for i in range(2)]
        pq = [ec(nc.psum_tensor(f"pq{i}", [128, 512], F32)) for i in range(2)]
        pav = [ec(nc.psum_tensor(f"pav{i}", [128, 512], F32)) for i in range(4)]

        SEMS = {s_name: ec(nc.semaphore(s_name)) for s_name in SEM_NAMES}
        blk = ec(nc.Block())
        S = Sched()
        Lsw = DmaQ([SEMS["dA"], SEMS["dB"], SEMS["dC"]])
        Lhw = DmaQ([SEMS["eA"], SEMS["eB"], SEMS["eC"]])
        CN = {k: 0 for k in SEM_NAMES if k[0] in "mcv" and "_" in k}
        on = S.on

        def inc(k, d=1):
            CN[k] += d
            return CN[k]

        def hw_snap():
            return [c * 16 for c in Lhw.counts]

        def emit_fence(e, sems_list, snap):
            for sem, val in zip(sems_list, snap):
                if val:
                    e.wait_ge(sem, val)

        Lsw.epoch()
        swc = [Lsw.plan(idbf, ident_d), Lsw.plan(b2sb, b2_d)]
        tok_swc = Lsw.close()

        @on("gpsimd")
        def _(e, plan=swc):
            for p in plan:
                p(e)

        Lhw.epoch()
        hwc = [Lhw.plan(id32, ident_d), Lhw.plan(bq_sb, bq_d), Lhw.plan(bk_sb, bk_d),
               Lhw.plan(bv_sb, bv_d), Lhw.plan(bp_sb, bp_d), Lhw.plan(b1_sb, b1_d),
               Lhw.plan(gb_sb, gb_d), Lhw.plan(delta8_sb, delta8_d),
               Lhw.plan(iota8c_sb, iota8c_d)]
        tok_hwc = Lhw.close()

        @on("sync")
        def _(e, plan=hwc):
            for p in plan:
                p(e)

        @on("vector")
        def _(e):
            e.memset(onesrow, 1.0)
            e.memset(eps_sb, 1e-5)
            e.memset(negC_sb, -C_SHIFT)
            e.drain()
            e.nop().then_inc(SEMS["v_ms"], 1)
        inc("v_ms")

        A16.mark("A")
        lnw_b = A16.alloc(D, "lnw")
        lnb_b = A16.alloc(D, "lnb")
        gw_b = A16.alloc(64, "gw")
        wqb = A16.alloc(8 * 1024, "wq")
        wkb = A16.alloc(8 * 1024, "wk")
        wvb = A16.alloc(8 * 1024, "wv")
        xt32 = [A32.alloc(1024, f"xt{i}") for i in range(2)]
        xn_tmb = [A16.alloc(1024, f"xntm{i}") for i in range(2)]
        zn_tmb = [A16.alloc(1024, f"zntm{i}") for i in range(2)]
        znf = [A16.alloc(8 * 512, f"znf{i}") for i in range(2)]
        xnf1 = [A16.alloc(8 * 512, f"xnf1_{i}") for i in range(2)]
        st8 = A16.alloc(8 * 512, "st8")
        vst2 = [A16.alloc(512, f"vst2_{i}") for i in range(2)]
        qst2 = [A16.alloc(512, f"qst2_{i}") for i in range(2)]
        rsc_fm = A32.alloc(512, "rsc_fm", rows=8)

        Lsw.epoch()
        sww = [Lsw.plan(lnw_b, lnw_d), Lsw.plan(lnb_b, lnb_d),
               Lsw.plan(gw_b.rearrange("p (c g) -> p c g", g=8),
                        gw_d.rearrange("(c p) g -> p c g", p=128))]
        for kc in range(8):
            sww.append(Lsw.plan(wqb[:, kc * 1024:(kc + 1) * 1024], wq_d[kc * 128:(kc + 1) * 128, :]))
            sww.append(Lsw.plan(wkb[:, kc * 1024:(kc + 1) * 1024], wk_d[kc * 128:(kc + 1) * 128, :]))
            sww.append(Lsw.plan(wvb[:, kc * 1024:(kc + 1) * 1024], wv_d[kc * 128:(kc + 1) * 128, :]))
        tok_sww = Lsw.close()

        @on("gpsimd")
        def _(e, plan=sww):
            for p in plan:
                p(e)

        prev_tok_xs = [None, None]
        prev_tok_zs = [None, None]
        tile_zs_toks = []
        prev_vtoks = []
        all_vtoks = []
        prev_qtoks = []
        prev_tok_k = [None]
        for u in range(32):
            T, sub = divmod(u, 4)
            own = T >= 4
            Lhw.epoch()
            pxl = Lhw.plan(xt32[u % 2], x_in[u * 128:(u + 1) * 128, :])
            tok_x = Lhw.close()

            @on("sync")
            def _(e, p=pxl, need=2 * max(0, u - 1)):
                if need:
                    e.wait_ge(SEMS["v_ln"], need)
                p(e)

            @on("vector")
            def _(e, u=u, tok=tok_x, buf=xt32[u % 2], ms=CN["v_ms"], cl=CN["c_ln"]):
                Lhw.wait(e, tok)
                e.wait_ge(SEMS["v_ms"], ms)
                if cl:
                    e.wait_ge(SEMS["c_ln"], cl - 1)
                e.bn_stats(mvst[u % 2][:, 0:6], buf[:, 0:512])
                e.bn_stats(mvst[u % 2][:, 6:12], buf[:, 512:1024])
                e.drain()
                e.bn_aggr(mv_sb[u % 2], mvst[u % 2])
                e.drain()
                e.nop().then_inc(SEMS["v_ln"], 1)
            inc("v_ln")

            @on("scalar")
            def _(e, u=u, need=CN["v_ln"]):
                e.wait_ge(SEMS["v_ln"], need)
                e.activation(out=rstd_sb[u % 2][:, 0:1], in_=mv_sb[u % 2][:, 1:2],
                             func=AF.Ln, bias=eps_sb)
                e.drain()
                e.activation(out=rstd_sb[u % 2][:, 0:1], in_=rstd_sb[u % 2][:, 0:1],
                             func=AF.Exp, scale=-0.5).then_inc(SEMS["c_ln"], 1)
            inc("c_ln")

            @on("vector")
            def _(e, u=u, buf=xt32[u % 2], outb=xn_tmb[u % 2], need=CN["c_ln"],
                  ptok=prev_tok_xs[u % 2]):
                e.wait_ge(SEMS["c_ln"], need)
                if ptok is not None:
                    Lhw.wait(e, ptok)
                e.tensor_scalar(out=outb, in0=buf, scalar1=mv_sb[u % 2][:, 0:1],
                                scalar2=rstd_sb[u % 2][:, 0:1],
                                op0=ALU.subtract, op1=ALU.mult)
                e.drain()
                e.bn_stats(mvst[u % 2][:, 0:6], outb[:, 0:512])
                e.bn_stats(mvst[u % 2][:, 6:12], outb[:, 512:1024])
                e.drain()
                e.bn_aggr(mv_sb[u % 2], mvst[u % 2])
                e.drain()
                e.nop().then_inc(SEMS["v_ln"], 1)
            inc("v_ln")

            @on("scalar")
            def _(e, u=u, need=CN["v_ln"]):
                e.wait_ge(SEMS["v_ln"], need)
                e.activation(out=rstd_sb[u % 2][:, 1:2], in_=mv_sb[u % 2][:, 1:2],
                             func=AF.Ln, bias=eps_sb)
                e.drain()
                e.activation(out=rstd_sb[u % 2][:, 1:2], in_=rstd_sb[u % 2][:, 1:2],
                             func=AF.Exp, scale=-0.5).then_inc(SEMS["c_ln"], 1)
            inc("c_ln")

            @on("vector")
            def _(e, u=u, need=CN["c_ln"], ptz=prev_tok_zs[u % 2], tokc=tok_sww):
                e.wait_ge(SEMS["c_ln"], need)
                if ptz is not None:
                    Lhw.wait(e, ptz)
                Lsw.wait(e, tokc)
                e.tensor_scalar(out=zn_tmb[u % 2], in0=xn_tmb[u % 2],
                                scalar1=mv_sb[u % 2][:, 0:1],
                                scalar2=rstd_sb[u % 2][:, 1:2],
                                op0=ALU.subtract, op1=ALU.mult)
                e.drain()
                e.tensor_tensor(out=zn_tmb[u % 2], in0=zn_tmb[u % 2], in1=lnw_b,
                                op=ALU.mult)
                e.drain()
                e.tensor_tensor(out=zn_tmb[u % 2], in0=zn_tmb[u % 2], in1=lnb_b,
                                op=ALU.add)
                e.drain()
                e.nop().then_inc(SEMS["v_ln"], 1)
            inc("v_ln")

            Lhw.epoch()
            pxs = Lhw.plan(xn_dram[u * 128:(u + 1) * 128, :], xn_tmb[u % 2])
            tok_xs = Lhw.close()
            Lhw.epoch()
            pzs = Lhw.plan(zn_dram[u * 128:(u + 1) * 128, :], zn_tmb[u % 2])
            tok_zs = Lhw.close()

            @on("sync")
            def _(e, p1=pxs, p2=pzs, need=CN["v_ln"]):
                e.wait_ge(SEMS["v_ln"], need - 1)
                p1(e)
                e.wait_ge(SEMS["v_ln"], need)
                p2(e)
            prev_tok_xs[u % 2] = tok_xs
            prev_tok_zs[u % 2] = tok_zs
            tile_zs_toks.append(tok_zs)

            if sub < 3:
                continue

            Lhw.epoch()
            pts = [Lhw.plan(znf[T % 2][:, kc * 512:(kc + 1) * 512],
                            zn_dram[T * 512:(T + 1) * 512, kc * 128:(kc + 1) * 128],
                            transpose=True) for kc in range(8)]
            tok_znf = Lhw.close()
            if own:
                Lhw.epoch()
                ptx = [Lhw.plan(xnf1[T % 2][:, kc * 512:(kc + 1) * 512],
                                xn_dram[T * 512:(T + 1) * 512, kc * 128:(kc + 1) * 128],
                                transpose=True) for kc in range(8)]
                tok_xnf1 = Lhw.close()
            else:
                tok_xnf1 = None

            @on("sync")
            def _(e, plan=pts, T=T, pjn=CN["m_pj"], rgn=CN["m_rg"],
                  zstoks=list(tile_zs_toks[-4:])):
                for tk in zstoks:
                    Lhw.wait(e, tk)
                if T >= 2:
                    e.wait_ge(SEMS["m_pj"], pjn - 24)
                    if T >= 6:
                        e.wait_ge(SEMS["m_rg"], rgn)
                for p in plan:
                    p(e)

            if own:
                @on("sync")
                def _(e, plan=ptx):
                    for p in plan:
                        p(e)

            Gp = CN["m_pj"]

            @on("tensor")
            def _(e, T=T, tok=tok_znf, Gp=Gp, tokw=tok_sww):
                Lhw.wait(e, tok)
                Lsw.wait(e, tokw)
                zn_t = znf[T % 2]
                for g in range(24):
                    pi, mc2 = divmod(g, 8)
                    wsb = (wkb, wvb, wqb)[pi]
                    G = Gp + g
                    if G >= 2:
                        e.wait_ge(SEMS["c_pj"], G - 1)
                    for kc in range(8):
                        ins = e.matmul(
                            pp[G % 2][:, :],
                            wsb[:, kc * 1024 + mc2 * 128: kc * 1024 + (mc2 + 1) * 128],
                            zn_t[:, kc * 512:(kc + 1) * 512],
                            start=(kc == 0), stop=(kc == 7))
                    ins.then_inc(SEMS["m_pj"], 1)
            inc("m_pj", 24)

            vtoks = []
            vplans = []
            qtoks = []
            qplans = []
            for mc2 in range(8):
                Lhw.epoch()
                vplans.append(Lhw.plan(
                    vfm_dram[mc2 * 128:(mc2 + 1) * 128, T * 512:(T + 1) * 512],
                    vst2[mc2 % 2]))
                vtoks.append(Lhw.close())
            all_vtoks.append(vtoks)
            if own:
                for mc2 in range(8):
                    Lhw.epoch()
                    qplans.append(Lhw.plan(
                        qfm_dram[mc2 * 128:(mc2 + 1) * 128,
                                 (T - 4) * 512:(T - 3) * 512],
                        qst2[mc2 % 2]))
                    qtoks.append(Lhw.close())

            @on("scalar")
            def _(e, T=T, Gp=Gp, own=own, vtoks=vtoks, qtoks=qtoks,
                  pvt=list(prev_vtoks), pqt=list(prev_qtoks), ptk=prev_tok_k[0]):
                for g in range(24):
                    pi, mc2 = divmod(g, 8)
                    G = Gp + g
                    e.wait_ge(SEMS["m_pj"], G + 1)
                    if pi == 0:
                        if g == 0 and ptk is not None:
                            Lhw.wait(e, ptk)
                        dst_ap = st8[:, mc2 * 512:(mc2 + 1) * 512]
                        bias = bk_sb[:, mc2:mc2 + 1]
                    elif pi == 1:
                        if mc2 >= 2:
                            Lhw.wait(e, vtoks[mc2 - 2])
                        elif pvt:
                            Lhw.wait(e, pvt[6 + mc2])
                        dst_ap = vst2[mc2 % 2]
                        bias = bv_sb[:, mc2:mc2 + 1]
                    else:
                        if own:
                            if mc2 >= 2:
                                Lhw.wait(e, qtoks[mc2 - 2])
                            elif pqt:
                                Lhw.wait(e, pqt[6 + mc2])
                            dst_ap = qst2[mc2 % 2]
                        else:
                            dst_ap = qst2[mc2 % 2]
                        bias = bq_sb[:, mc2:mc2 + 1]
                    e.activation(out=dst_ap, in_=pp[G % 2][:, :], func=AF.Identity,
                                 bias=bias).then_inc(SEMS["c_pj"], 1)
            inc("c_pj", 24)

            @on("sync")
            def _(e, Gp=Gp, vplans=vplans, qplans=qplans, cp0=CN["c_pj"] - 24, own=own):
                for mc2 in range(8):
                    e.wait_ge(SEMS["c_pj"], cp0 + 8 + mc2 + 1)
                    vplans[mc2](e)
                if own:
                    for mc2 in range(8):
                        e.wait_ge(SEMS["c_pj"], cp0 + 16 + mc2 + 1)
                        qplans[mc2](e)
            prev_vtoks = vtoks
            if own:
                prev_qtoks = qtoks

            Lhw.epoch()
            pk = Lhw.plan(
                kfm_dram.rearrange("(c p) t -> p c t", p=128)[:, :, T * 512:(T + 1) * 512],
                st8.rearrange("p (c t) -> p c t", c=8))
            tok_k = Lhw.close()

            @on("sync")
            def _(e, pk=pk, need=CN["c_pj"] - 16):
                e.wait_ge(SEMS["c_pj"], need)
                pk(e)
            prev_tok_k[0] = tok_k

            if own:
                Gr = CN["m_rg"]

                @on("tensor")
                def _(e, T=T, Gr=Gr, tokx=tok_xnf1, rg_ev=CN["c_rg"]):
                    Lhw.wait(e, tokx)
                    if Gr >= 1:
                        e.wait_ge(SEMS["c_rg"], Gr)
                    xn_t = xnf1[T % 2]
                    for kc in range(8):
                        i1 = e.matmul(pav[0][0:8, :], gw_b[:, kc * 8:(kc + 1) * 8],
                                      xn_t[:, kc * 512:(kc + 1) * 512],
                                      start=(kc == 0), stop=(kc == 7))
                    i1.then_inc(SEMS["m_rg"], 1)
                inc("m_rg", 1)

                @on("scalar")
                def _(e, base=CN["m_rg"] - 1, tr_prev=CN["m_tr"]):
                    Lhw.wait(e, tok_hwc)
                    e.wait_ge(SEMS["m_rg"], base + 1)
                    if tr_prev:
                        e.wait_ge(SEMS["m_tr"], tr_prev)
                    e.activation(out=rsc_fm, in_=pav[0][0:8, :], func=AF.Identity,
                                 bias=gb_sb).then_inc(SEMS["c_rg"], 1)
                inc("c_rg", 1)

                Tr = CN["m_tr"]

                @on("tensor")
                def _(e, Tr=Tr, need=CN["c_rg"], tm_prev=CN["v_tm"]):
                    e.wait_ge(SEMS["c_rg"], need)
                    for i in range(4):
                        Tg = Tr + i
                        if Tg >= 2:
                            e.wait_ge(SEMS["v_tm"], Tg - 1)
                        e.transpose(pav[2 + (i % 2)][0:128, 0:8],
                                    rsc_fm[0:8, i * 128:(i + 1) * 128],
                                    id32[0:8, 0:8]).then_inc(SEMS["m_tr"], 1)
                inc("m_tr", 4)

                @on("vector")
                def _(e, Tr=Tr, T=T):
                    for i in range(4):
                        sub4 = (T - 4) * 4 + i
                        e.wait_ge(SEMS["m_tr"], Tr + i + 1)
                        e.tensor_copy(gl_tm[:, sub4 * 8:sub4 * 8 + 8],
                                      pav[2 + (i % 2)][0:128, 0:8]).then_inc(SEMS["v_tm"], 1)
                    e.drain()
                inc("v_tm", 4)

        for s16 in range(16):
            gsl = gl_tm[:, s16 * 8:(s16 + 1) * 8]

            @on("vector")
            def _(e, gsl=gsl, cm=CN["c_ms"], first=(s16 == 0), vt=CN["v_tm"]):
                if first:
                    Lhw.wait(e, tok_hwc)
                    e.wait_ge(SEMS["v_tm"], vt)
                if cm:
                    e.wait_ge(SEMS["c_ms"], cm)
                e.tensor_reduce(out=m8[:, 0:1], in_=gsl, axis=AX.X, op=ALU.max)
                e.drain()
                e.tensor_scalar(out=m8[:, 1:2], in0=m8[:, 0:1], scalar1=-1.0,
                                scalar2=None, op0=ALU.mult)
                e.drain()
                e.nop().then_inc(SEMS["v_tm"], 1)
            inc("v_tm")

            @on("scalar")
            def _(e, gsl=gsl, need=CN["v_tm"]):
                e.wait_ge(SEMS["v_tm"], need)
                e.activation(out=k8a, in_=gsl, func=AF.Exp,
                             bias=m8[:, 1:2]).then_inc(SEMS["c_ms"], 1)
            inc("c_ms")

            @on("vector")
            def _(e, need=CN["c_ms"]):
                e.wait_ge(SEMS["c_ms"], need)
                e.tensor_reduce(out=m8[:, 2:3], in_=k8a, axis=AX.X, op=ALU.add)
                e.drain()
                e.nop().then_inc(SEMS["v_tm"], 1)
            inc("v_tm")

            @on("scalar")
            def _(e, need=CN["v_tm"]):
                e.wait_ge(SEMS["v_tm"], need)
                e.activation(out=m8[:, 3:4], in_=m8[:, 2:3], func=AF.Ln)
                e.drain()
                e.activation(out=m8[:, 3:4], in_=m8[:, 3:4], func=AF.Exp,
                             scale=-1.0).then_inc(SEMS["c_ms"], 1)
            inc("c_ms")

            @on("vector")
            def _(e, s16=s16, need=CN["c_ms"]):
                e.wait_ge(SEMS["c_ms"], need)
                e.tensor_scalar(out=k8a, in0=k8a, scalar1=m8[:, 3:4], scalar2=None,
                                op0=ALU.mult)
                e.drain()
                e.tensor_tensor(out=k8b, in0=k8a, in1=delta8_sb, op=ALU.subtract)
                e.drain()
                e.tensor_reduce(out=m8[:, 4:5], in_=k8b, axis=AX.X, op=ALU.max)
                e.drain()
                e.tensor_scalar(out=k8c, in0=k8b, scalar1=m8[:, 4:5], scalar2=None,
                                op0=ALU.is_ge)
                e.drain()
                e.scalar_tensor_tensor(out=k8b, in0=k8c, scalar=-1e30, in1=k8b,
                                       op0=ALU.mult, op1=ALU.add)
                e.drain()
                e.tensor_reduce(out=m8[:, 5:6], in_=k8b, axis=AX.X, op=ALU.max)
                e.drain()
                e.tensor_scalar(out=k8b, in0=k8b, scalar1=m8[:, 5:6], scalar2=None,
                                op0=ALU.is_ge)
                e.drain()
                e.tensor_tensor(out=k8b, in0=k8b, in1=k8c, op=ALU.add)
                e.drain()
                e.tensor_tensor(out=k8c, in0=k8b, in1=iota8c_sb, op=ALU.mult)
                e.drain()
                e.tensor_reduce(out=m8[:, 6:7], in_=k8c, axis=AX.X, op=ALU.max)
                e.drain()
                e.tensor_scalar(out=k8c, in0=iota8c_sb, scalar1=m8[:, 6:7],
                                scalar2=None, op0=ALU.is_ge)
                e.drain()
                e.tensor_tensor(out=k8c, in0=k8c, in1=k8b, op=ALU.mult)
                e.drain()
                e.tensor_copy(selw_tm[:, s16 * 8:(s16 + 1) * 8], k8c)
                e.tensor_tensor(out=k8a, in0=k8a, in1=k8c, op=ALU.mult)
                e.drain()
                e.tensor_copy(wisel_tm[:, s16 * 8:(s16 + 1) * 8], k8a)
                e.drain()
                e.nop().then_inc(SEMS["v_tm"], 1)
            inc("v_tm")

        TrS = CN["m_tr"]

        @on("tensor")
        def _(e, TrS=TrS, vt=CN["v_tm"], vff0=CN["v_ff"]):
            e.wait_ge(SEMS["v_tm"], vt)
            for t in range(16):
                if t >= 2:
                    e.wait_ge(SEMS["v_ff"], vff0 + (t - 2) + 1)
                e.transpose(pav[t % 2][0:8, 0:128], selw_tm[:, t * 8:(t + 1) * 8],
                            id32).then_inc(SEMS["m_tr"], 1)
        inc("m_tr", 16)

        @on("vector")
        def _(e, TrS=TrS):
            for t in range(16):
                e.wait_ge(SEMS["m_tr"], TrS + t + 1)
                e.tensor_copy(selT[0:8, t * 128:(t + 1) * 128],
                              pav[t % 2][0:8, 0:128]).then_inc(SEMS["v_ff"], 1)
            e.drain()
            e.nop().then_inc(SEMS["v_ff"], 1)
        inc("v_ff", 17)

        if stage == "A":
            Lhw.epoch()
            pdbg = [Lhw.plan(dbg["wisel"], wisel_tm), Lhw.plan(dbg["selw"], selw_tm)]
            tok_dbg = Lhw.close()

            @on("sync")
            def _(e, plan=pdbg, vt=CN["v_tm"], vf=CN["v_ff"]):
                e.wait_ge(SEMS["v_tm"], vt)
                e.wait_ge(SEMS["v_ff"], vf)
                for p in plan:
                    p(e)
                emit_fence(e, Lhw.sems, [c * 16 for c in Lhw.counts])
                emit_fence(e, Lsw.sems, [c * 16 for c in Lsw.counts])
            S.emit(blk)
            return nc

        A16.reset("A")
        vstage = [A16.alloc(1024, f"vstg{i}") for i in range(2)]
        ones_f = A16.alloc(1040, "ones_f")

        @on("vector")
        def _(e, need=CN["v_tm"]):
            e.wait_ge(SEMS["v_tm"], need)
            e.memset(ones_f, 1.0)
            e.drain()
            e.nop().then_inc(SEMS["v_cc"], 1)
        inc("v_cc")

        Lsw.epoch()
        p_ones = [Lsw.plan(vtm_dram[t * 128:(t + 1) * 128, :], ones_f) for t in range(32)]
        tok_ones = Lsw.close()

        @on("gpsimd")
        def _(e, plan=p_ones, need=CN["v_cc"]):
            e.wait_ge(SEMS["v_cc"], need)
            for p in plan:
                p(e)

        for t in range(32):
            Lhw.epoch()
            pvt = Lhw.plan(vstage[t % 2], vfm_dram[:, t * 128:(t + 1) * 128],
                           transpose=True)
            tok_vl = Lhw.close()

            @on("sync")
            def _(e, t=t, p=pvt, ptv=list(all_vtoks[-1]), vstoks=all_vtoks,
                  ccn=CN["c_cc"]):
                if t == 0:
                    for vl in vstoks:
                        for tk in vl:
                            Lhw.wait(e, tk)
                if t >= 2:
                    e.wait_ge(SEMS["c_cc"], ccn)
                p(e)

            Lhw.epoch()
            pvs = Lhw.plan(vtm_dram[t * 128:(t + 1) * 128, :].rearrange(
                "r (h c) -> r h c", c=65)[:, :, 0:64],
                vstage[t % 2].rearrange("r (h c) -> r h c", c=64))
            tok_vs = Lhw.close()

            @on("sync")
            def _(e, t=t, p=pvs, tok=tok_vl, tko=tok_ones):
                if t == 0:
                    Lsw.wait(e, tko)
                Lhw.wait(e, tok)
                p(e)

            @on("scalar")
            def _(e, tok=tok_vs):
                Lhw.wait(e, tok)
                e.nop().then_inc(SEMS["c_cc"], 1)
            inc("c_cc")

        A16.reset("A")
        k_res = A16.alloc(8 * S_FULL, "k_res")
        v_res = A16.alloc(32 * 1040, "v_res")
        q_qb = A16.alloc(8 * 512, "q_qb")
        acc_qb = A16.alloc(8 * 512, "acc_qb")
        s_st8 = q_qb
        wpb = A16.alloc(8 * 1024, "wproj")
        e_b = [A16.alloc(512, f"e{i}") for i in range(2)]
        av_sb = A16.alloc(512, "av_sb", rows=64)

        snap_C = hw_snap()
        Lhw.epoch()
        pkv = [Lhw.plan(k_res[:, kc * S_FULL // 8 * 8:(kc + 1) * 4096] if False else
                        k_res[:, kc * 4096:(kc + 1) * 4096],
                        kfm_dram[kc * 128:(kc + 1) * 128, :]) for kc in range(8)]
        pkv += [Lhw.plan(v_res[:, t * 1040:(t + 1) * 1040],
                         vtm_dram[t * 128:(t + 1) * 128, :]) for t in range(32)]
        tok_kv = Lhw.close()

        @on("sync")
        def _(e, plan=pkv, snap=snap_C, ccn=CN["c_cc"]):
            e.wait_ge(SEMS["c_cc"], ccn)
            emit_fence(e, Lhw.sems, snap)
            for p in plan:
                p(e)

        Lsw.epoch()
        pwp = [Lsw.plan(wpb[:, kc * 1024:(kc + 1) * 1024],
                        wp_d[kc * 128:(kc + 1) * 128, :]) for kc in range(8)]
        tok_wp = Lsw.close()

        @on("gpsimd")
        def _(e, plan=pwp):
            for p in plan:
                p(e)

        prev_tok_s = [None]
        prev_tok_q = [None]
        for qb in range(4):
            Lhw.epoch()
            pq_l = [Lhw.plan(q_qb[:, kc * 512:(kc + 1) * 512],
                             qfm_dram[kc * 128:(kc + 1) * 128,
                                      qb * 512:(qb + 1) * 512]) for kc in range(8)]
            tok_q = Lhw.close()

            @on("sync")
            def _(e, plan=pq_l, qb=qb, msn=CN["m_sc"], qts=list(prev_qtoks),
                  pts_=prev_tok_s[0]):
                if qb == 0:
                    for tk in qts:
                        Lhw.wait(e, tk)
                else:
                    e.wait_ge(SEMS["m_sc"], msn)
                    if pts_ is not None:
                        Lhw.wait(e, pts_)
                for p in plan:
                    p(e)

            for h in range(16):
                hr = (h % 2) * 64
                kc_h = h // 2
                qsl = q_qb[hr:hr + 64, kc_h * 512:(kc_h + 1) * 512]
                Gs = CN["m_sc"]
                Ga = CN["m_av"]
                Cx = CN["c_ex"]
                Vm = CN["v_ep"]
                Cv = CN["c_av"]
                Bc = CN["m_bc"]

                @on("tensor")
                def _(e, h=h, qb=qb, qsl=qsl, kc_h=kc_h, hr=hr, Gs=Gs, Ga=Ga,
                      tokq=tok_q, tokkv=tok_kv, Vm=Vm, Cv=Cv, epn=CN["v_ep"]):
                    Lhw.wait(e, tokq)
                    if h == 0 and qb == 0:
                        Lhw.wait(e, tokkv)
                    if h > 0 or qb > 0:
                        e.wait_ge(SEMS["v_ep"], epn)
                        e.wait_ge(SEMS["c_av"], Cv)
                    def av_mm(c):
                        e.wait_ge(SEMS["c_ex"], Gs + c + 1)
                        e.matmul(pav[0][0:65, :],
                                 v_res[:, c * 1040 + h * 65: c * 1040 + (h + 1) * 65],
                                 e_b[c % 2][0:128, :],
                                 start=(c == 0), stop=(c == 31)).then_inc(SEMS["m_av"], 1)
                    for c in range(32):
                        G = Gs + c
                        if G >= 2:
                            e.wait_ge(SEMS["c_ex"], G - 1)
                        e.matmul(pp[G % 2][:, :],
                                 k_res[hr:hr + 64, kc_h * 4096 + c * 128:
                                       kc_h * 4096 + (c + 1) * 128],
                                 qsl, start=True, stop=True).then_inc(SEMS["m_sc"], 1)
                        if c >= 1:
                            av_mm(c - 1)
                    av_mm(31)
                inc("m_sc", 32)
                inc("m_av", 32)

                @on("scalar")
                def _(e, Gs=Gs, Ga=Ga, h=h, qb=qb):
                    for c in range(32):
                        e.wait_ge(SEMS["m_sc"], Gs + c + 1)
                        if c >= 2:
                            e.wait_ge(SEMS["m_av"], Ga + c - 1)
                        e.activation(out=e_b[c % 2][0:128, :], in_=pp[(Gs + c) % 2][:, :],
                                     func=AF.Exp, bias=negC_sb,
                                     scale=SCALE).then_inc(SEMS["c_ex"], 1)
                inc("c_ex", 32)

                accsl = acc_qb[hr:hr + 64, kc_h * 512:(kc_h + 1) * 512]

                @on("vector")
                def _(e, Ga=Ga, Cv=Cv, Bc=Bc, accsl=accsl, h=h, qb=qb, won=CN["m_wo"]):
                    if qb > 0 and h == 0:
                        e.wait_ge(SEMS["m_wo"], won)
                    e.wait_ge(SEMS["m_av"], Ga + 32)
                    e.drain()
                    e.tensor_scalar(out=den_g, in0=pav[0][64:65, :], scalar1=1e-18,
                                    scalar2=None, op0=ALU.max)
                    e.drain()
                    e.nop().then_inc(SEMS["v_ep"], 1)
                    e.wait_ge(SEMS["c_av"], Cv + 1)
                    e.wait_ge(SEMS["m_bc"], Bc + 1)
                    e.wait_ge(SEMS["c_av"], Cv + 2)
                    e.tensor_tensor(out=accsl, in0=av_sb, in1=pav[3][0:64, :],
                                    op=ALU.mult)
                    e.drain()
                    e.nop().then_inc(SEMS["v_ep"], 1)
                inc("v_ep", 2)
                inc("c_av", 2)

                @on("scalar")
                def _(e, Ve=Vm, Ga=Ga):
                    e.wait_ge(SEMS["v_ep"], Ve + 1)
                    e.activation(out=den_g, in_=den_g, func=AF.Ln)
                    e.drain()
                    e.activation(out=r_bf, in_=den_g, func=AF.Exp,
                                 scale=-1.0).then_inc(SEMS["c_av"], 1)
                    e.drain()
                    e.activation(out=av_sb, in_=pav[0][0:64, :],
                                 func=AF.Copy).then_inc(SEMS["c_av"], 1)

                @on("tensor")
                def _(e, Ve=Vm, Cv=Cv, Bc=Bc):
                    e.wait_ge(SEMS["c_av"], Cv + 1)
                    if Bc >= 1:
                        e.wait_ge(SEMS["v_ep"], Ve + 2 * 0 + 0 if False else Ve)
                    e.matmul(pav[3][0:64, :], onesrow, r_bf,
                             start=True, stop=True).then_inc(SEMS["m_bc"], 1)
                inc("m_bc", 1)

            Gw = CN["m_wo"]

            @on("tensor")
            def _(e, qb=qb, tokw=tok_wp, epn=CN["v_ep"], can=CN["c_ms"]):
                Lsw.wait(e, tokw)
                e.wait_ge(SEMS["v_ep"], epn)
                for mo in range(8):
                    if mo >= 2:
                        e.wait_ge(SEMS["c_ms"], can + mo - 2 + 1)
                    for kc in range(8):
                        ins = e.matmul(pq[mo % 2][:, :],
                                       wpb[:, kc * 1024 + mo * 128: kc * 1024 + (mo + 1) * 128],
                                       acc_qb[:, kc * 512:(kc + 1) * 512],
                                       start=(kc == 0), stop=(kc == 7))
                    ins.then_inc(SEMS["m_wo"], 1)
            inc("m_wo", 8)

            @on("scalar")
            def _(e, Gw=Gw, ptoks=prev_tok_s[0]):
                if ptoks is not None:
                    Lhw.wait(e, ptoks)
                for mo in range(8):
                    e.wait_ge(SEMS["m_wo"], Gw + mo + 1)
                    e.activation(out=s_st8[:, mo * 512:(mo + 1) * 512],
                                 in_=pq[mo % 2][:, :], func=AF.Identity,
                                 bias=bp_sb[:, mo:mo + 1]).then_inc(SEMS["c_ms"], 1)
            inc("c_ms", 8)

            Lhw.epoch()
            ps_st = Lhw.plan(
                sfm_dram.rearrange("(c p) t -> p c t", p=128)[:, :, qb * 512:(qb + 1) * 512],
                s_st8.rearrange("p (c t) -> p c t", c=8))
            tok_s = Lhw.close()

            @on("sync")
            def _(e, p=ps_st, need=CN["c_ms"]):
                e.wait_ge(SEMS["c_ms"], need)
                p(e)
            prev_tok_s[0] = tok_s

        if stage == "C":
            @on("sync")
            def _(e, wv=CN["v_ep"], wc=CN["c_ms"]):
                e.wait_ge(SEMS["v_ep"], wv)
                e.wait_ge(SEMS["c_ms"], wc)
                emit_fence(e, Lhw.sems, [c * 16 for c in Lhw.counts])
                emit_fence(e, Lsw.sems, [c * 16 for c in Lsw.counts])
            S.emit(blk)
            return nc

        A16.reset("persist")
        xnT = [A16.alloc(1024, f"xnT{i}") for i in range(2)]
        xsE = [A16.alloc(8 * 512, f"xsE{i}") for i in range(2)]
        w1sb = A16.alloc(8 * 2048, "w1sb")
        w2sb = A16.alloc(16 * 1024, "w2sb")
        hsb = A16.alloc(16 * 512, "hsb")
        accf = A16.alloc(16 * 1024, "accf")
        stm = A16.alloc(1024, "stm")
        xn1t = A16.alloc(1024, "xn1t")

        EB = dict(m_wo=CN["m_wo"], c_ms=CN["c_ms"], v_ep=CN["v_ep"])

        R8 = CN["m_im"]

        @on("tensor")
        def _(e, R8=R8, vff=CN["v_ff"], tokb=tok_swc, EB=EB):
            e.wait_ge(SEMS["v_ep"], EB["v_ep"])
            e.wait_ge(SEMS["c_ms"], EB["c_ms"])
            e.wait_ge(SEMS["v_ff"], vff)
            Lsw.wait(e, tokb)
            for i in range(32):
                ts, dh = divmod(i, 2)
                if i >= 2:
                    e.wait_ge(SEMS["v_ff"], vff + (i - 2) + 1)
                e.matmul(pav[i % 2][0:128, 0:512], selT[0:8, ts * 128:(ts + 1) * 128],
                         b2sb[:, dh * 512:(dh + 1) * 512],
                         start=True, stop=True).then_inc(SEMS["m_im"], 1)
        inc("m_im", 32)

        @on("vector")
        def _(e, R8=R8, EB=EB, ts_tok=prev_tok_s[0]):
            e.wait_ge(SEMS["v_ep"], EB["v_ep"])
            Lhw.wait(e, ts_tok)
            for i in range(32):
                ts, dh = divmod(i, 2)
                e.wait_ge(SEMS["m_im"], R8 + i + 1)
                e.tensor_copy(accf[:, ts * 1024 + dh * 512: ts * 1024 + (dh + 1) * 512],
                              pav[i % 2][0:128, 0:512]).then_inc(SEMS["v_ff"], 1)
            e.drain()
            e.nop().then_inc(SEMS["v_ff"], 1)
        inc("v_ff", 33)

        prev_xs_toks = []
        for e_i in range(8):
            xs_toks = []
            for t16 in range(16):
                Lhw.epoch()
                pxl = Lhw.plan(xnT[t16 % 2], xn_dram[2048 + t16 * 128: 2048 + (t16 + 1) * 128, :])
                tok_xn = Lhw.close()

                @on("sync")
                def _(e, p=pxl, t16=t16, e_i=e_i, vffn=CN["v_ff"], EB=EB,
                      ts_tok=prev_tok_s[0]):
                    if t16 >= 2 or e_i > 0:
                        e.wait_ge(SEMS["v_ff"], vffn - 1)
                    else:
                        e.wait_ge(SEMS["m_wo"], EB["m_wo"])
                        e.wait_ge(SEMS["v_ep"], EB["v_ep"])
                        e.wait_ge(SEMS["c_ms"], EB["c_ms"])
                        Lhw.wait(e, ts_tok)
                    p(e)

                @on("vector")
                def _(e, t16=t16, e_i=e_i, tok=tok_xn,
                      pxt=(prev_xs_toks[t16] if (e_i > 0) else None)):
                    Lhw.wait(e, tok)
                    if pxt is not None:
                        Lhw.wait(e, pxt)
                    e.drain()
                    e.tensor_scalar(out=xnT[t16 % 2], in0=xnT[t16 % 2],
                                    scalar1=wisel_tm[:, t16 * 8 + e_i: t16 * 8 + e_i + 1],
                                    scalar2=None, op0=ALU.mult)
                    e.drain()
                    e.nop().then_inc(SEMS["v_ff"], 1)
                inc("v_ff")

                Lhw.epoch()
                pxst = Lhw.plan(xs_dram[t16 * 128:(t16 + 1) * 128, :], xnT[t16 % 2])
                xs_toks.append(Lhw.close())

                @on("sync")
                def _(e, p=pxst, need=CN["v_ff"]):
                    e.wait_ge(SEMS["v_ff"], need)
                    p(e)
            prev_xs_toks = xs_toks

            for hh in range(2):
                phase = e_i * 2 + hh
                need_f2 = CN["m_f2"]
                need_f1 = CN["m_f1"]
                Lsw.epoch()
                planw = [Lsw.plan(w1sb[:, kc * 2048:(kc + 1) * 2048],
                                  w1_d[e_i, kc * 128:(kc + 1) * 128,
                                       hh * 2048:(hh + 1) * 2048])
                         for kc in range(8)]
                planw += [Lsw.plan(w2sb[:, hc * 1024:(hc + 1) * 1024],
                                   w2_d[e_i, hh * 2048 + hc * 128:
                                        hh * 2048 + (hc + 1) * 128, :])
                          for hc in range(16)]
                tok_w = Lsw.close()

                @on("gpsimd")
                def _(e, phase=phase, planw=planw, f1=need_f1, f2=need_f2, EB=EB):
                    if phase > 0:
                        e.wait_ge(SEMS["m_f1"], f1)
                        e.wait_ge(SEMS["m_f2"], f2)
                    else:
                        e.wait_ge(SEMS["m_wo"], EB["m_wo"])
                        e.wait_ge(SEMS["v_ep"], EB["v_ep"])
                        e.wait_ge(SEMS["c_ms"], EB["c_ms"])
                    for p in planw:
                        p(e)

                for tt in range(4):
                    jglob = phase * 4 + tt
                    Lhw.epoch()
                    pxn = [Lhw.plan(xsE[jglob % 2][:, kc * 512:(kc + 1) * 512],
                                    xs_dram[tt * 512:(tt + 1) * 512,
                                            kc * 128:(kc + 1) * 128], transpose=True)
                           for kc in range(8)]
                    tok_xsE = Lhw.close()

                    @on("sync")
                    def _(e, plan=pxn, jglob=jglob, f1n=CN["m_f1"],
                          xtoks=list(xs_toks[tt * 4:(tt + 1) * 4])):
                        for tk in xtoks:
                            Lhw.wait(e, tk)
                        if jglob >= 2:
                            e.wait_ge(SEMS["m_f1"], f1n - 16)
                        for p in plan:
                            p(e)

                    F1 = CN["m_f1"]

                    @on("tensor")
                    def _(e, tt=tt, tok=tok_w, tokx=tok_xsE, F1=F1, ch=CN["c_h"],
                          vff=CN["v_ff"], jglob=jglob):
                        Lsw.wait(e, tok)
                        Lhw.wait(e, tokx)
                        e.wait_ge(SEMS["v_ff"], vff)
                        for m in range(16):
                            G = F1 + m
                            if G >= 2:
                                e.wait_ge(SEMS["c_h"], G - 1)
                            for kc in range(8):
                                ins = e.matmul(
                                    pp[G % 2][:, :],
                                    w1sb[:, kc * 2048 + m * 128: kc * 2048 + (m + 1) * 128],
                                    xsE[jglob % 2][:, kc * 512:(kc + 1) * 512],
                                    start=(kc == 0), stop=(kc == 7))
                            ins.then_inc(SEMS["m_f1"], 1)
                    inc("m_f1", 16)

                    @on("scalar")
                    def _(e, e_i=e_i, hh=hh, F1=F1):
                        Lhw.wait(e, tok_hwc)
                        for m in range(16):
                            e.wait_ge(SEMS["m_f1"], F1 + m + 1)
                            e.activation(out=hsb[:, m * 512:(m + 1) * 512],
                                         in_=pp[(F1 + m) % 2][:, :], func=AF.Relu,
                                         bias=b1_sb[:, e_i * 32 + hh * 16 + m:
                                                    e_i * 32 + hh * 16 + m + 1],
                                         scale=1.0).then_inc(SEMS["c_h"], 1)
                    inc("c_h", 16)

                    F2 = CN["m_f2"]

                    @on("tensor")
                    def _(e, tt=tt, ch=CN["c_h"], F2=F2, vff=CN["v_ff"]):
                        e.wait_ge(SEMS["c_h"], ch)
                        for g in range(8):
                            ts, dh = divmod(g, 2)
                            G2 = F2 + g
                            if G2 >= 2:
                                e.wait_ge(SEMS["v_ff"], vff + (g - 2) + 1 if g >= 2 else 0)
                            for m in range(16):
                                ins = e.matmul(
                                    pav[G2 % 2][0:128, 0:512],
                                    hsb[:, m * 512 + ts * 128: m * 512 + (ts + 1) * 128],
                                    w2sb[:, m * 1024 + dh * 512: m * 1024 + (dh + 1) * 512],
                                    start=(m == 0), stop=(m == 15))
                            ins.then_inc(SEMS["m_f2"], 1)
                    inc("m_f2", 8)

                    @on("vector")
                    def _(e, tt=tt, F2=F2, e_i=e_i):
                        for g in range(8):
                            ts, dh = divmod(g, 2)
                            gts = tt * 4 + ts
                            e.wait_ge(SEMS["m_f2"], F2 + g + 1)
                            e.drain()
                            e.scalar_tensor_tensor(
                                out=accf[:, gts * 1024 + dh * 512: gts * 1024 + (dh + 1) * 512],
                                in0=pav[(F2 + g) % 2][0:128, 0:512],
                                scalar=selw_tm[:, gts * 8 + e_i: gts * 8 + e_i + 1],
                                in1=accf[:, gts * 1024 + dh * 512: gts * 1024 + (dh + 1) * 512],
                                op0=ALU.mult, op1=ALU.add).then_inc(SEMS["v_ff"], 1)
                    inc("v_ff", 8)

        prev_tok_o = [None]
        for st in range(16):
            Lhw.epoch()
            pfx = [Lhw.plan(xf32, x_in[2048 + st * 128: 2048 + (st + 1) * 128, :]),
                   Lhw.plan(stm, sfm_dram[:, st * 128:(st + 1) * 128], transpose=True),
                   Lhw.plan(xn1t, xn_dram[2048 + st * 128: 2048 + (st + 1) * 128, :])]
            tok_f = Lhw.close()

            @on("sync")
            def _(e, plan=pfx, st=st, vfin=CN["v_fin"], pto=prev_tok_o[0]):
                if st > 0:
                    e.wait_ge(SEMS["v_fin"], vfin)
                if pto is not None:
                    Lhw.wait(e, pto)
                for p in plan:
                    p(e)

            @on("scalar")
            def _(e, st=st, vff=CN["v_ff"], vfin=CN["v_fin"]):
                e.wait_ge(SEMS["v_ff"], vff)
                if st > 0:
                    e.wait_ge(SEMS["v_fin"], vfin)
                e.activation(out=sil32, in_=accf[:, st * 1024:(st + 1) * 1024],
                             func=AF.Silu).then_inc(SEMS["c_sl"], 1)
            inc("c_sl")

            @on("vector")
            def _(e, st=st, tok=tok_f, need=CN["c_sl"]):
                Lhw.wait(e, tok)
                e.wait_ge(SEMS["c_sl"], need)
                e.tensor_tensor(out=stm, in0=stm, in1=xn1t, op=ALU.add)
                e.drain()
                e.tensor_tensor(out=t32, in0=stm, in1=sil32, op=ALU.mult)
                e.drain()
                e.tensor_tensor(out=t32, in0=t32, in1=xf32, op=ALU.add)
                e.drain()
                e.nop().then_inc(SEMS["v_fin"], 1)
            inc("v_fin")

            Lhw.epoch()
            pout = Lhw.plan(x2_out[st * 128:(st + 1) * 128, :], t32)
            tok_o = Lhw.close()

            @on("sync")
            def _(e, p=pout, need=CN["v_fin"]):
                e.wait_ge(SEMS["v_fin"], need)
                p(e)
            prev_tok_o[0] = tok_o

        @on("sync")
        def _(e):
            emit_fence(e, Lhw.sems, [c * 16 for c in Lhw.counts])
            emit_fence(e, Lsw.sems, [c * 16 for c in Lsw.counts])

        S.emit(blk)
        return nc



def kernel(**inputs):
    from concourse.bass_utils import run_bass_kernel_spmd
    cores = list(range(8))

    nc0 = build_l0("F")
    maps0 = [l0_in_map(inputs, c // 2, c % 2) for c in cores]
    r0 = run_bass_kernel_spmd(nc0, maps0, core_ids=cores)
    x1_full = np.empty((4, S_FULL, D), np.float32)
    for c in cores:
        be, hf = c // 2, c % 2
        x1_full[be, hf * S_OWN:(hf + 1) * S_OWN] = r0.results[c]["x1"]

    nc1 = build_l1("F")
    maps1 = [l1_in_map(inputs, x1_full, c // 2, c % 2) for c in cores]
    r1 = run_bass_kernel_spmd(nc1, maps1, core_ids=cores)
    out = np.empty((4, S_FULL, D), np.float32)
    for c in cores:
        be, hf = c // 2, c % 2
        out[be, hf * S_OWN:(hf + 1) * S_OWN] = r1.results[c]["x2"]
    return out
